# version 06
# baseline (speedup 1.0000x reference)
import numpy as np
import ml_dtypes
import concourse.bass as bass
import concourse.mybir as mybir
from concourse.bass_utils import run_bass_kernel_spmd
from contextlib import ExitStack

F32 = mybir.dt.float32
BF16 = mybir.dt.bfloat16
AF = mybir.ActivationFunctionType
ALU = mybir.AluOpType
AXX = mybir.AxisListType.X

D = 2048
DK = 128
NH = 16
NKV = 4
S = 2048
B = 4
I_FF = 5632
NT = 9
TLOC = NT * 128
TMAIN = 1024
NCORE = 8
W1SL = 1408
W2SL = 256
WQSL = 256
WKSL = 64
MODSL = 1536
MAGIC = float(1.5 * 2 ** 23)
MASKF = -1.0e9
LN_EPS = 1e-5
RMS_EPS = 1e-8
Q_EPS = 1e-8
QMAX = 127.0
KTW = 1216
KB = 1024
DEV = {}


def _perm_head(dk=DK):
    return np.concatenate([np.arange(0, dk, 2), np.arange(1, dk, 2)])


def _perm_full(nheads, dk=DK):
    p = _perm_head(dk)
    return np.concatenate([h * dk + p for h in range(nheads)])


def _rope_tables(s0):
    inv = 1.0 / (10000.0 ** (np.arange(0, DK, 2, dtype=np.float64) / DK))
    pos = np.maximum(s0 + np.arange(TLOC, dtype=np.float64), 0.0)
    ph = pos[:, None] * inv[None, :]
    cos = np.cos(ph).astype(np.float32)
    sin = np.sin(ph).astype(np.float32)
    return (np.ascontiguousarray(np.tile(cos, (1, NH))),
            np.ascontiguousarray(np.tile(sin, (1, NH))),
            np.ascontiguousarray(np.tile(cos, (1, NKV))),
            np.ascontiguousarray(np.tile(sin, (1, NKV))))


def _mask_tile(first_kill_halo):
    i = np.arange(128)[:, None]
    k = np.arange(256)[None, :]
    ok = (k >= i) & (k <= i + 64) & (k < 192)
    if first_kill_halo:
        ok = ok & (k >= 64)
    one = np.where(ok, np.float32(0.0), np.float32(MASKF)).astype(np.float32)
    return np.ascontiguousarray(np.tile(one, (1, 8)))


class Sem:
    def __init__(self, es, nc, name):
        self.h = es.enter_context(nc.semaphore(name))
        self.n = 0
        self.name = name


class Plan:
    ENGS = ("sync", "vector", "scalar", "tensor", "gpsimd")

    def __init__(self, es, nc):
        self.nc = nc
        self.plan = {e: [] for e in self.ENGS}
        self.sem = {e: Sem(es, nc, "s_" + e) for e in self.ENGS}
        self.counts = {e: 0 for e in self.ENGS}

    def _waits(self, eng, toks, force=False):
        best = {}
        for t in toks:
            if t is None:
                continue
            sem, v, peng, small = t
            if (not force) and peng == eng and not small:
                continue
            cur = best.get(id(sem))
            if cur is None or cur[1] < v:
                best[id(sem)] = (sem, v)
        for sem, v in best.values():
            assert v <= sem.n, f"wait on future {sem.name} {v}>{sem.n}"
            self.plan[eng].append(lambda e, s=sem, vv=v: e.wait_ge(s.h, vv))

    def op(self, eng, fn, toks=(), small=False, force_wait=False):
        self._waits(eng, toks, force=force_wait)
        sem = self.sem[eng]
        sem.n += 1
        self.plan[eng].append(lambda e, f=fn, s=sem: f(e).then_inc(s.h, 1))
        self.counts[eng] += 1
        return (sem, sem.n, eng, small)

    def dma(self, queue, out, in_, toks=(), **kw):
        self._waits(queue, toks, force=True)
        sem = self.sem[queue]
        sem.n += 16
        self.plan[queue].append(
            lambda e, o=out, i=in_, s=sem, k=kw: e.dma_start(out=o, in_=i, **k)
            .then_inc(s.h, 16))
        self.counts[queue] += 1
        return (sem, sem.n, queue, False)

    def raw(self, eng, fn):
        self.plan[eng].append(fn)


def build_nc(tap=None, until=None):
    nc = bass.Bass("TRN2", target_bir_lowering=False)
    dt = nc.dram_tensor

    xin = dt("xin", [TLOC, D], F32, kind="ExternalInput")
    cfull = dt("cfull", [B, D], F32, kind="ExternalInput")
    wmod_sl = dt("wmod_sl", [D, MODSL], F32, kind="ExternalInput")
    w1_sl = dt("w1_sl", [W1SL, D], F32, kind="ExternalInput")
    w2_sl = dt("w2_sl", [W2SL, I_FF], F32, kind="ExternalInput")
    wq_sl = dt("wq_sl", [WQSL, D], F32, kind="ExternalInput")
    wk_sl = dt("wk_sl", [WKSL, D], F32, kind="ExternalInput")
    wv_sl = dt("wv_sl", [WKSL, D], F32, kind="ExternalInput")
    wo_sl = dt("wo_sl", [WQSL, D], F32, kind="ExternalInput")
    bq_r = dt("bq_r", [1, D], F32, kind="ExternalInput")
    bk_r = dt("bk_r", [1, NKV * DK], F32, kind="ExternalInput")
    bv_r = dt("bv_r", [1, NKV * DK], F32, kind="ExternalInput")
    bo_r = dt("bo_r", [1, D], F32, kind="ExternalInput")
    qn16 = dt("qn16", [1, D], F32, kind="ExternalInput")
    kn4 = dt("kn4", [1, NKV * DK], F32, kind="ExternalInput")
    scal = dt("scal", [1, 1], F32, kind="ExternalInput")
    sel = dt("sel", [32, 8], F32, kind="ExternalInput")
    identb = dt("identb", [128, 128], BF16, kind="ExternalInput")
    identf = dt("identf", [128, 128], F32, kind="ExternalInput")
    onesr = dt("onesr", [1, 128], F32, kind="ExternalInput")
    cosq_t = dt("cosq_t", [TLOC, NH * 64], F32, kind="ExternalInput")
    sinq_t = dt("sinq_t", [TLOC, NH * 64], F32, kind="ExternalInput")
    cosk_t = dt("cosk_t", [TLOC, NKV * 64], F32, kind="ExternalInput")
    sink_t = dt("sink_t", [TLOC, NKV * 64], F32, kind="ExternalInput")
    mask_1 = dt("mask_1", [128, 8 * 256], F32, kind="ExternalInput")
    mask_r = dt("mask_r", [128, 8 * 256], F32, kind="ExternalInput")

    o_out = dt("o_out", [TMAIN, D], F32, kind="ExternalOutput")

    ag_mod_in = dt("ag_mod_in", [B, MODSL], F32)
    ag_mod_out = dt("ag_mod_out", [NCORE * B, MODSL], F32)
    ag_w1_in = dt("ag_w1_in", [D, W1SL], BF16)
    ag_w1_out = dt("ag_w1_out", [NCORE * D, W1SL], BF16)
    ag_w2_in = dt("ag_w2_in", [1, I_FF * W2SL], BF16)
    ag_w2_out = dt("ag_w2_out", [NCORE, I_FF * W2SL], BF16)
    ag_wq_in = dt("ag_wq_in", [D, WQSL], BF16)
    ag_wq_out = dt("ag_wq_out", [NCORE * D, WQSL], BF16)
    ag_wk_in = dt("ag_wk_in", [D, WKSL], BF16)
    ag_wk_out = dt("ag_wk_out", [NCORE * D, WKSL], BF16)
    ag_wv_in = dt("ag_wv_in", [D, WKSL], BF16)
    ag_wv_out = dt("ag_wv_out", [NCORE * D, WKSL], BF16)
    ag_wo_in = dt("ag_wo_in", [D, WQSL], BF16)
    ag_wo_out = dt("ag_wo_out", [NCORE * D, WQSL], BF16)
    modrow_d = dt("modrow_d", [1, 6 * D], F32)

    es = ExitStack()
    with es:
        trPS = es.enter_context(nc.psum_tensor("trPS", [128, 2048], BF16))
        big4 = es.enter_context(nc.psum_tensor("big4", [128, 2048], F32))
        acc2 = es.enter_context(nc.psum_tensor("acc2", [128, 1024], F32))
        dn67 = trPS[:, 0:2048].bitcast(F32)

        sbuf_base = (nc._sbuf_addr_for_side("left") + 1023) & ~1023
        sbuf_cap = nc.sbuf_bytes_remaining - 2048
        uid = [0]

        def mk(name, shape, dtype, off_b):
            uid[0] += 1
            per = int(np.prod(shape[1:])) * mybir.dt.size(dtype)
            assert off_b + per <= sbuf_cap, f"{name}: {off_b}+{per} > {sbuf_cap}"
            return nc.alloc_sbuf_tensor_at(f"{name}_{uid[0]}", list(shape), dtype,
                                           offset=sbuf_base + off_b)

        P = Plan(es, nc)
        sCC = [Sem(es, nc, f"sCC{i}") for i in range(7)]

        def _finish():
            P.raw("gpsimd", lambda e: e.wait_ge(P.sem["gpsimd"].h, P.sem["gpsimd"].n))
            P.raw("sync", lambda e: e.wait_ge(P.sem["sync"].h, P.sem["sync"].n))
            with nc.Block() as block:
                for eng in Plan.ENGS:
                    getattr(block, eng)(lambda E, eng=eng: [f(E) for f in P.plan[eng]])
            build_nc.last_stats = {"counts": dict(P.counts),
                                   "sems": {e: P.sem[e].n for e in Plan.ENGS}}

        def cc(idx, ag_in, ag_out, toks):
            P._waits("gpsimd", toks, force=True)
            s = sCC[idx]
            s.n += 1
            P.plan["gpsimd"].append(
                lambda e, a=ag_in, b=ag_out, ss=s: e.collective_compute(
                    "AllGather", ALU.bypass, replica_groups=[list(range(NCORE))],
                    ins=[a.ap().opt()], outs=[b.ap().opt()]).then_inc(ss.h, 1))
            return (s, s.n, "gpsimd", False)

        poff = [0]

        def pk(name, shape, dtype):
            h = mk(name, shape, dtype, poff[0])
            per = int(np.prod(shape[1:])) * mybir.dt.size(dtype)
            poff[0] += (per + 31) & ~31
            return h

        identb_s = pk("identb", [128, 128], BF16)
        identf_s = pk("identf", [128, 128], F32)
        onesr_s = pk("onesr", [1, 128], F32)
        modt = [pk(f"modt{i}", [128, D], BF16) for i in range(2)]
        gat_f = pk("gat_f", [128, D], F32)
        s1t = pk("s1t", [128, 1], F32)
        s2t = pk("s2t", [128, 1], F32)
        mt = pk("mt", [128, 1], F32)
        nmt = pk("nmt", [128, 1], F32)
        vart = pk("vart", [128, 1], F32)
        rstdt = pk("rstdt", [128, 1], F32)
        ms16 = pk("ms16", [128, 16], F32)
        rs16 = pk("rs16", [128, 16], F32)
        gmax = pk("gmax", [128, 8], F32)
        ngmax = pk("ngmax", [128, 8], F32)
        gsum = pk("gsum", [128, 8], F32)
        grec = pk("grec", [128, 8], F32)
        mx1 = pk("mx1", [128, 1], F32)
        sc1 = pk("sc1", [128, 1], F32)
        rc1 = pk("rc1", [128, 1], F32)
        sc1a = pk("sc1a", [128, 1], F32)
        rc1a = pk("rc1a", [128, 1], F32)
        scalb = pk("scalb", [128, 1], F32)
        row1 = pk("row1", [1, 512], F32)
        tiny2 = pk("tiny2", [1, 512], F32)
        tinyb = pk("tinyb", [1, 512], F32)
        epsln_t = pk("epsln_t", [128, 1], F32)
        epsrm_t = pk("epsrm_t", [128, 1], F32)
        R0 = (poff[0] + 1023) & ~1023

        def rg(name, shape, dtype, off_kb):
            return mk(name, shape, dtype, R0 + int(off_kb * KB))

        t_epsa = P.op("vector", lambda e: e.memset(epsln_t[:, :], float(LN_EPS)),
                      small=True)
        t_epsb = P.op("vector", lambda e: e.memset(epsrm_t[:, :], float(RMS_EPS)),
                      small=True)
        t_idb = P.dma("sync", identb_s[:, :], identb[:, :])
        t_idf = P.dma("sync", identf_s[:, :], identf[:, :])
        t_one = P.dma("sync", onesr_s[:, :], onesr[:, :])

        def bcast_row(dst_ap, src_row_ap, width, toks, extra_mm=()):
            t_ld = P.dma("sync", row1[0:1, 0:width], src_row_ap, toks=toks)
            mm = P.op("tensor", lambda e, w=width: e.matmul(
                big4[:, 1536:1536 + w], onesr_s[0:1, :], row1[0:1, 0:w],
                start=True, stop=True), toks=[t_ld, t_one] + list(extra_mm))
            return P.op("vector", lambda e, w=width: e.tensor_copy(
                dst_ap, big4[:, 1536:1536 + w]), toks=[mm])

        def ln_chain(src_ap, dump_f32, dump_b16, toks, eps):
            t_s1 = P.op("vector", lambda e: e.tensor_scalar(
                dump_f32, src_ap, 1.0, 0.0, ALU.mult, ALU.add,
                accum_out=s1t[:, :]), toks=toks, small=True)
            t_s2 = P.op("scalar", lambda e: e.activation(
                dump_b16, src_ap, AF.Square, accum_out=s2t[:, :]),
                toks=toks, small=True)
            t_m = P.op("vector", lambda e: e.tensor_scalar(
                mt[:, :], s1t[:, :], 1.0 / D, None, ALU.mult),
                toks=[t_s1, t_s2], small=True)
            t_nm = P.op("vector", lambda e: e.tensor_scalar(
                nmt[:, :], mt[:, :], -1.0, None, ALU.mult), toks=[t_m], small=True)
            t_v1 = P.op("vector", lambda e: e.scalar_tensor_tensor(
                vart[:, :], mt[:, :], -1.0, mt[:, :], ALU.mult, ALU.mult),
                toks=[t_nm], small=True)
            t_v2 = P.op("vector", lambda e: e.tensor_scalar(
                s2t[:, :], s2t[:, :], 1.0 / D, None, ALU.mult), toks=[t_v1], small=True)
            t_v3 = P.op("vector", lambda e: e.tensor_add(
                vart[:, :], vart[:, :], s2t[:, :]), toks=[t_v2], small=True)
            t_ln = P.op("scalar", lambda e: e.activation(
                vart[:, :], vart[:, :], AF.Ln, bias=epsln_t[:, :]),
                toks=[t_v3, t_epsa], small=True)
            t_rs = P.op("scalar", lambda e: e.activation(
                rstdt[:, :], vart[:, :], AF.Exp, scale=-0.5), toks=[t_ln], small=True)
            return t_nm, t_rs

        csb = rg("csb", [B, D], F32, 0)
        silc = rg("silc", [B, D], F32, 8)
        silcT = rg("silcT", [128, 64], F32, 16)
        wmodb = [rg(f"wmodb{i}", [128, MODSL], F32, 16.5 + 6 * i) for i in range(2)]
        gmod_s = rg("gmod_s", [32, MODSL], F32, 28.5)
        selb = rg("selb", [32, 8], F32, 34.5)
        mod_sb = rg("mod_sb", [B, MODSL], F32, 35)
        prep_bf = rg("prep_bf", [128, I_FF], BF16, 41)
        stag = rg("stag", [128, 44 * 128], BF16, 52.5)
        prep_in = [rg(f"prep_in{i}", [128, I_FF], F32, 127 + 22 * i) for i in range(2)]

        t_c = P.dma("sync", csb[:, :], cfull[:, :])
        t_sel = P.dma("sync", selb[:, :], sel[:, :])

        t_sil = P.op("scalar", lambda e: e.activation(silc[:, :], csb[:, :], AF.Silu),
                     toks=[t_c])
        trt = [P.op("tensor", lambda e, k=k: e.transpose(
            acc2[:, 4 * k:4 * k + 4], silc[0:4, 128 * k:128 * (k + 1)], identf_s[0:4, 0:4]),
            toks=[t_sil, t_idf]) for k in range(16)]
        t_silT = P.op("vector", lambda e: e.tensor_copy(silcT[:, :], acc2[:, 0:64]),
                      toks=trt)

        mod_mms = []
        wm_tok = [None, None]
        for k in range(16):
            buf = wmodb[k % 2]
            war = mod_mms[3 * (k - 2) + 2] if k >= 2 else None
            wm_tok[k % 2] = P.dma("sync", buf[:, :],
                                  wmod_sl.ap()[128 * k:128 * (k + 1), :], toks=[war])
            for u in range(3):
                t = P.op("tensor", lambda e, k=k, u=u, b=buf: e.matmul(
                    big4[0:4, 512 * u:512 * (u + 1)], silcT[:, 4 * k:4 * k + 4],
                    b[:, 512 * u:512 * (u + 1)], start=(k == 0), stop=(k == 15)),
                    toks=[wm_tok[k % 2], t_silT] if u == 0 else [])
                mod_mms.append(t)
        t_mev = P.op("vector", lambda e: e.tensor_copy(mod_sb[:, :], big4[0:4, 0:1536]),
                     toks=[mod_mms[-1]])
        t_mw = P.dma("gpsimd", ag_mod_in.ap(), mod_sb[:, :], toks=[t_mev])
        t_ccmod = cc(0, ag_mod_in, ag_mod_out, [t_mw])
        if until == "p1":
            _finish()
            return nc

        def prep_transpose(src_dram, nrows, ag_in):
            outs = []
            for oc in range((nrows + 127) // 128):
                rr = min(128, nrows - 128 * oc)
                t_ld = P.dma("gpsimd", prep_bf[0:rr, 0:D],
                             src_dram.ap()[128 * oc:128 * oc + rr, :],
                             toks=outs[-1:])
                trs = [P.op("tensor", lambda e, k=k, rr=rr: e.transpose(
                    trPS[:, 128 * k:128 * k + rr], prep_bf[0:rr, 128 * k:128 * (k + 1)],
                    identb_s[0:rr, 0:rr]), toks=[t_ld, t_idb]) for k in range(16)]
                t_ev = P.op("vector", lambda e, rr=rr: e.tensor_copy(
                    stag[:, 0:16 * rr].rearrange("p (k j) -> p k j", k=16),
                    trPS[:, 0:2048].rearrange("p (k j) -> p k j", k=16)[:, :, 0:rr]),
                    toks=trs)
                t_st = P.dma("gpsimd",
                             ag_in.ap().rearrange("(k p) j -> p k j", p=128)
                             [:, :, 128 * oc:128 * oc + rr],
                             stag[:, 0:16 * rr].rearrange("p (k j) -> p k j", k=16),
                             toks=[t_ev])
                outs.append(t_st)
            return outs

        t_ccwq = cc(1, ag_wq_in, ag_wq_out, prep_transpose(wq_sl, WQSL, ag_wq_in))
        t_ccwk = cc(2, ag_wk_in, ag_wk_out, prep_transpose(wk_sl, WKSL, ag_wk_in))
        t_ccwv = cc(3, ag_wv_in, ag_wv_out, prep_transpose(wv_sl, WKSL, ag_wv_in))
        t_ccwo = cc(4, ag_wo_in, ag_wo_out, prep_transpose(wo_sl, WQSL, ag_wo_in))
        if until == "p2":
            _finish()
            return nc

        def quant_rows(buf, ncols, toks):
            t_mx = P.op("vector", lambda e, n=ncols: e.reduce_max(
                mx1[:, :], buf[0:128, 0:n], axis=AXX, apply_absolute_value=True),
                toks=toks, small=True)
            t_sc = P.op("vector", lambda e: e.tensor_scalar(
                sc1[:, :], mx1[:, :], 1.0 / QMAX, float(Q_EPS), ALU.mult, ALU.max),
                toks=[t_mx], small=True)
            t_rc = P.op("vector", lambda e: e.reciprocal(rc1[:, :], sc1[:, :]),
                        toks=[t_sc], small=True)
            t_ca = P.op("scalar", lambda e: e.activation(
                rc1a[:, :], rc1[:, :], AF.Identity), toks=[t_rc], small=True)
            t_cb = P.op("scalar", lambda e: e.activation(
                sc1a[:, :], sc1[:, :], AF.Identity), toks=[t_rc], small=True)
            t_r1 = P.op("vector", lambda e, n=ncols: e.tensor_scalar(
                buf[0:128, 0:n], buf[0:128, 0:n], rc1a[:, :], MAGIC, ALU.mult, ALU.add),
                toks=[t_ca])
            return P.op("vector", lambda e, n=ncols: e.tensor_scalar(
                prep_bf[0:128, 0:n], buf[0:128, 0:n], -MAGIC, sc1a[:, :],
                ALU.add, ALU.mult), toks=[t_r1, t_cb], force_wait=True)

        w1_st = []
        prev_ev = None
        for ti in range(11):
            buf = prep_in[ti % 2]
            war = w1_st[ti - 2] if ti >= 2 else (
                t_ccwo if DEV.get("serial_prep") else None)
            t_ld = P.dma("sync", buf[:, 0:D], w1_sl.ap()[128 * ti:128 * (ti + 1), :],
                         toks=[war])
            t_q = quant_rows(buf, D, [t_ld, prev_ev] if prev_ev else [t_ld])
            stag_war = w1_st[-1] if w1_st else t_ccwo
            trs = [P.op("tensor", lambda e, k=k: e.transpose(
                trPS[:, 128 * k:128 * (k + 1)], prep_bf[:, 128 * k:128 * (k + 1)],
                identb_s[:, :]),
                toks=[t_q] + ([prev_ev, stag_war] if k == 0 else []))
                for k in range(16)]
            prev_ev = P.op("vector", lambda e: e.tensor_copy(
                stag[:, 0:2048], trPS[:, 0:2048]), toks=trs)
            w1_st.append(P.dma(
                "gpsimd",
                ag_w1_in.ap().rearrange("(k p) j -> p k j", p=128)
                [:, :, 128 * ti:128 * (ti + 1)],
                stag[:, 0:2048].rearrange("p (k j) -> p k j", k=16),
                toks=[prev_ev]))
        if until == "p2a":
            _finish()
            return nc
        t_ccw1 = cc(5, ag_w1_in, ag_w1_out, w1_st)
        if until == "p2b":
            _finish()
            return nc

        w2_st = []
        t_ld_w2 = None
        for ti in range(2 if not DEV.get("w2_tiles") else DEV["w2_tiles"]):
            buf = prep_in[ti % 2]
            t_ld = P.dma("sync", buf[:, 0:I_FF],
                         w2_sl.ap()[128 * ti:128 * (ti + 1), :],
                         toks=[w1_st[-1]] if ti == 0 else [w2_st[-1]])
            t_ld_w2 = t_ld
            t_q = quant_rows(buf, I_FF, [t_ld, prev_ev])
            if DEV.get("w2_noquant"):
                t_q = prev_ev
            ev_toks = []
            if DEV.get("w2_notrans"):
                w2_st.append(t_ld)
                prev_ev = t_q
                continue
            stag_war2 = (w2_st[-1] if w2_st else w1_st[-1])
            for gi in range(3):
                k0 = 16 * gi
                n = min(16, 44 - k0)
                war = ev_toks[gi - 1] if gi >= 1 else prev_ev
                trg = [P.op("tensor", lambda e, kk=k0 + j, p=j: e.transpose(
                    trPS[:, 128 * p:128 * (p + 1)], prep_bf[:, 128 * kk:128 * (kk + 1)],
                    identb_s[:, :]),
                    toks=[t_q, war] + ([stag_war2] if gi == 0 else [])
                    if j == 0 else [])
                    for j in range(n)]
                ev_toks.append(P.op("vector", lambda e, k0=k0, n=n: e.tensor_copy(
                    stag[:, 128 * k0:128 * (k0 + n)],
                    trPS[:, 0:128 * n]), toks=trg))
            prev_ev = ev_toks[-1]
            if DEV.get("w2_nodma"):
                w2_st.append(t_ld)
                continue
            for q4 in range(4):
                kk0 = 11 * q4
                w2_st.append(P.dma(
                    "gpsimd",
                    bass.AP(ag_w2_in, 128 * ti + W2SL * 128 * kk0,
                            [[W2SL, 128], [W2SL * 128, 11], [1, 128]]),
                    stag[:, 128 * kk0:128 * (kk0 + 11)]
                    .rearrange("p (k j) -> p k j", k=11),
                    toks=[ev_toks[-1]]))
        if until == "p2c":
            _finish()
            return nc
        t_ccw2 = cc(6, ag_w2_in, ag_w2_out, w2_st)
        if until == "p3":
            _finish()
            return nc

        t_gm = P.dma("sync", gmod_s[:, :], ag_mod_out.ap(),
                     toks=[t_ccmod, t_ld_w2])
        prev_t = None
        t_md = None
        for r in range(8):
            mm3 = [P.op("tensor", lambda e, r=r, u=u: e.matmul(
                big4[0:1, 512 * u:512 * (u + 1)], selb[:, r:r + 1],
                gmod_s[:, 512 * u:512 * (u + 1)], start=True, stop=True),
                toks=[t_gm, t_sel, t_silT, prev_t, t_mev] if u == 0 else [])
                for u in range(3)]
            prev_t = P.op("vector", lambda e: e.tensor_copy(
                mod_sb[0:1, 0:1536], big4[0:1, 0:1536]), toks=mm3 + [t_mw])
            t_md = P.dma("gpsimd", modrow_d.ap()[0:1, MODSL * r:MODSL * (r + 1)],
                         mod_sb[0:1, 0:1536], toks=[prev_t])
            prev_t = t_md

        t_scb = bcast_row(scalb[:, :], scal.ap()[0:1, 0:1], 1, [t_md],
                          extra_mm=[t_mev])
        if until == "p4":
            _finish()
            return nc
        if until == "wcheck":
            P.dma("gpsimd", o_out.ap()[0:1024, 0:1408],
                  ag_w1_out.ap()[0:1024, 0:1408], toks=[t_ccw1])
            P.dma("gpsimd", o_out.ap()[0:1024, 1408:2048],
                  bass.AP(ag_w2_out, 0, [[640, 1024], [1, 640]]), toks=[t_ccw2])
            _finish()
            return nc

        qnt = rg("qnt", [128, D], BF16, 0)
        knt = rg("knt", [128, NKV * DK], BF16, 4)
        biasw = rg("biasw", [128, 512], F32, 5)
        biasw2 = rg("biasw2", [128, 512], F32, 7)
        vs = rg("vs", [128, 9 * 512], BF16, 9)
        q_tok = rg("q_tok", [128, 8 * D], BF16, 18)
        k_tok = rg("k_tok", [128, 9 * 512], BF16, 50)
        hT = rg("hT", [128, 9 * 2048], BF16, 59)
        wwina = rg("wwina", [128, 16 * 512], BF16, 95)
        wwinb = rg("wwinb", [128, 16 * 512], BF16, 111)
        xbuf = [rg(f"xbuf{i}", [128, D], F32, 127 + 8 * i) for i in range(2)]
        xn_s = rg("xn_s", [128, D], F32, 143)
        h_s = rg("h_s", [128, D], BF16, 151)
        hmu_s = rg("hmu_s", [128, D], BF16, 155)
        sqdump = rg("sqdump", [128, D], BF16, 159)
        vtmp = rg("vtmp", [128, 2 * 512], BF16, 163)
        qT = rg("qT", [128, 8 * 2048], BF16, 59)
        kT = rg("kT", [128, NKV * KTW], BF16, 91)
        ropeb = [rg(f"ropeb{i}", [128, NH * 64], F32, 101 + 4 * i) for i in range(4)]
        mscr = [rg(f"mscr{i}", [128, NH * 64], F32, 117 + 4 * i) for i in range(4)]
        sq_scr = rg("sq_scr", [128, D], F32, 133)
        attnT = rg("attnT", [128, 8 * 2048], BF16, 18)
        mask1_s = rg("mask1_s", [128, 2048], F32, 101)
        maskr_s = rg("maskr_s", [128, 2048], F32, 109)
        S_sb = rg("S_sb", [128, 2048], F32, 117)
        P_sb = rg("P_sb", [128, 2048], BF16, 125)
        PT_sb = rg("PT_sb", [128, 2048], BF16, 129)
        x2 = rg("x2", [128, 8 * D], F32, 50)
        wwoa = rg("wwoa", [128, 16 * 512], BF16, 114)
        wwob = rg("wwob", [128, 16 * 512], BF16, 130)
        xrel = [rg(f"xrel{i}", [128, D], F32, 146 + 8 * i) for i in range(2)]
        t1s = [rg(f"t1s{i}", [128, 512], F32, 162 + 2 * i) for i in range(2)]
        xqT = rg("xqT", [128, 4 * 2048], BF16, 0)
        w1ring = [rg(f"w1ring{i}", [128, 1024], BF16, 16 + 2 * i) for i in range(4)]
        w2ring = [rg(f"w2ring{i}", [128, 2048], BF16, 24 + 4 * i) for i in range(2)]
        h2_s = rg("h2_s", [128, D], F32, 32)
        hm2_s = rg("hm2_s", [128, D], BF16, 40)
        sgl4 = rg("sgl4", [128, 4 * 512], BF16, 44)
        rscr = rg("rscr", [128, 512], F32, 48)
        t1f = rscr
        hbuf = rg("hbuf", [128, 4 * I_FF], BF16, 114)
        hqT = [rg(f"hqT{i}", [128, 44 * 128], BF16, 158 + 11 * i) for i in range(2)]

        prev = t_scb
        qn_mms = []
        for u in range(4):
            t_l = P.dma("sync", tiny2[0:1, 0:512], qn16.ap()[0:1, 512 * u:512 * (u + 1)],
                        toks=[prev])
            prev = P.op("tensor", lambda e, u=u: e.matmul(
                big4[:, 512 * u:512 * (u + 1)], onesr_s[0:1, :], tiny2[0:1, 0:512],
                start=True, stop=True), toks=[t_l, t_scb])
            qn_mms.append(prev)
        t_qnt = P.op("vector", lambda e: e.tensor_scalar(
            qnt[:, :], big4[:, :], scalb[:, :], None, ALU.mult), toks=qn_mms)
        t_knl = P.dma("sync", tiny2[0:1, 0:512], kn4.ap(), toks=[t_qnt])
        kn_mm = P.op("tensor", lambda e: e.matmul(
            big4[:, 0:512], onesr_s[0:1, :], tiny2[0:1, 0:512], start=True, stop=True),
            toks=[t_knl, t_qnt])
        t_knt = P.op("vector", lambda e: e.tensor_copy(knt[:, :], big4[:, 0:512]),
                     toks=[kn_mm])

        def build_mod_tiles(which, toks_first):
            toks = list(toks_first)
            outs = []
            for j, logi in enumerate((0, 1, 2) if which == 0 else (3, 4, 5)):
                mms = []
                prev_mm = None
                for u in range(4):
                    t_ld = P.dma("sync", tiny2[0:1, 0:512],
                                 modrow_d.ap()[0:1, D * logi + 512 * u:
                                               D * logi + 512 * (u + 1)],
                                 toks=toks + [prev_mm])
                    src = t_ld
                    if j == 2:
                        t_abs = P.op("scalar", lambda e: e.activation(
                            tinyb[0:1, 0:512], tiny2[0:1, 0:512], AF.Abs), toks=[t_ld])
                        t_mx2 = P.op("vector", lambda e: e.tensor_scalar(
                            tinyb[0:1, 0:512], tinyb[0:1, 0:512], 0.25, None, ALU.max),
                            toks=[t_abs], small=True, force_wait=True)
                        t_ge = P.op("vector", lambda e: e.tensor_scalar(
                            tiny2[0:1, 0:512], tiny2[0:1, 0:512], 0.0, 2.0,
                            ALU.is_ge, ALU.mult),
                            toks=[t_mx2], small=True, force_wait=True)
                        t_sg = P.op("vector", lambda e: e.tensor_scalar(
                            tiny2[0:1, 0:512], tiny2[0:1, 0:512], -1.0, None, ALU.add),
                            toks=[t_ge], small=True, force_wait=True)
                        src = P.op("vector", lambda e: e.tensor_mul(
                            tiny2[0:1, 0:512], tiny2[0:1, 0:512], tinyb[0:1, 0:512]),
                            toks=[t_sg], small=True, force_wait=True)
                    prev_mm = P.op("tensor", lambda e, u=u: e.matmul(
                        big4[:, 512 * u:512 * (u + 1)], onesr_s[0:1, :],
                        tiny2[0:1, 0:512], start=True, stop=True),
                        toks=[src] + (toks if u == 0 else []))
                    mms.append(prev_mm)
                if j < 2:
                    t_cp = P.op("vector", lambda e, j=j: e.tensor_copy(
                        modt[j][:, :], big4[:, :]), toks=mms)
                else:
                    t_cp = P.op("vector", lambda e: e.tensor_copy(
                        gat_f[:, :], big4[:, :]), toks=mms)
                outs.append(t_cp)
                toks = [t_cp]
            return outs

        mt_toks = build_mod_tiles(0, [t_knt, t_md])

        if until == "prep":
            _finish()
            return nc

        def load_win_q(dst, G, w, toks):
            return [P.dma(
                "sync",
                dst[:, 0:16 * 512].rearrange("p (k j) -> p k j", k=16)
                [:, 8 * kh:8 * (kh + 1), 256 * rloc:256 * (rloc + 1)],
                G.ap().rearrange("(r k p) j -> p r k j", k=16, p=128)
                [:, 2 * w + rloc, 8 * kh:8 * (kh + 1), :],
                toks=toks) for rloc in range(2) for kh in range(2)]

        def load_win_kv(dst, G, toks):
            return [P.dma(
                "sync",
                dst[:, 512 * k:512 * (k + 1)].rearrange("p (r j) -> p r j", r=8),
                G.ap().rearrange("(r k p) j -> p k r j", k=16, p=128)[:, k, :, :],
                toks=toks) for k in range(16)]

        _serial = [t_ccw1, t_ccw2] if DEV.get("serial_prep") else []
        t_wk = load_win_kv(wwina, ag_wk_out, [t_ccwk, w2_st[-1]] + _serial)
        t_wv = load_win_kv(wwinb, ag_wv_out, [t_ccwv, w2_st[-1]])
        t_bk = bcast_row(biasw[:, 0:512], bk_r.ap()[0:1, 0:512], 512,
                         [mt_toks[-1]], extra_mm=[mt_toks[-1]])
        t_bv = bcast_row(biasw2[:, 0:512], bv_r.ap()[0:1, 0:512], 512, [t_bk])

        h_done = []
        hT_done = []
        kv_ev = {}
        vs_done = {}
        for ti in range(NT):
            buf = xbuf[ti % 2]
            war = [h_done[ti - 2]] if ti >= 2 else [w2_st[-1]]
            t_x = P.dma("sync", buf[:, :], xin.ap()[128 * ti:128 * (ti + 1), :],
                        toks=war)
            prev = [hT_done[ti - 1]] if ti >= 1 else (mt_toks + [w2_st[-1], t_scb])
            t_nm, t_rs = ln_chain(buf[:, :], xn_s[:, :], sqdump[:, :],
                                  [t_x] + prev, LN_EPS)
            t_xn = P.op("vector", lambda e, b=buf: e.tensor_scalar(
                xn_s[:, :], b[:, :], nmt[:, :], rstdt[:, :], ALU.add, ALU.mult),
                toks=[t_nm, t_rs])
            t_h1 = P.op("gpsimd", lambda e: e.tensor_mul(
                hmu_s[:, :], xn_s[:, :], modt[1][:, :]), toks=[t_xn], force_wait=True)
            t_h2 = P.op("gpsimd", lambda e: e.tensor_add(
                hmu_s[:, :], hmu_s[:, :], modt[0][:, :]), toks=[t_h1])
            t_h = P.op("vector", lambda e: e.tensor_add(
                h_s[:, :], hmu_s[:, :], xn_s[:, :]), toks=[t_h2])
            h_done.append(t_h)
            trs = [P.op("tensor", lambda e, k=k: e.transpose(
                trPS[:, 128 * k:128 * (k + 1)], h_s[:, 128 * k:128 * (k + 1)],
                identb_s[:, :]),
                toks=[t_h] + ([hT_done[ti - 1]] if (k == 0 and ti >= 1) else []))
                for k in range(16)]
            t_hT = P.op("vector", lambda e, ti=ti: e.tensor_copy(
                hT[:, 2048 * ti:2048 * (ti + 1)], trPS[:, 0:2048]), toks=trs)
            hT_done.append(t_hT)

            for which in ("k", "v"):
                wbuf = wwina if which == "k" else wwinb
                wtok = t_wk if which == "k" else t_wv
                sl = 0 if which == "k" else 512
                extra = [t_hT] + wtok
                prevkey = (ti, "k") if which == "v" else (ti - 1, "v")
                if prevkey in kv_ev:
                    extra.append(kv_ev[prevkey])
                mms = [P.op("tensor", lambda e, k=k, ti=ti, wb=wbuf, sl=sl: e.matmul(
                    acc2[:, sl:sl + 512],
                    hT[:, 2048 * ti + 128 * k:2048 * ti + 128 * (k + 1)],
                    wb[:, 512 * k:512 * (k + 1)],
                    start=(k == 0), stop=(k == 15)),
                    toks=extra if k == 0 else []) for k in range(16)]
                if which == "k":
                    kv_ev[(ti, "k")] = P.op("vector", lambda e, ti=ti: e.tensor_add(
                        k_tok[:, 512 * ti:512 * (ti + 1)], acc2[:, 0:512],
                        biasw[:, 0:512]), toks=mms + [t_bk])
                else:
                    kv_ev[(ti, "v")] = P.op("vector", lambda e, ti=ti: e.tensor_add(
                        vtmp[:, 512 * (ti % 2):512 * (ti % 2 + 1)], acc2[:, 512:1024],
                        biasw2[:, 0:512]),
                        toks=mms + [t_bv, vs_done.get(ti - 1)])
            if ti >= 1:
                pv = vtmp[:, 512 * ((ti - 1) % 2):512 * ((ti - 1) % 2 + 1)]
                cv = vtmp[:, 512 * (ti % 2):512 * (ti % 2 + 1)]
                P.op("vector", lambda e, pv=pv, ti=ti: e.tensor_copy(
                    vs[0:64, 512 * (ti - 1):512 * ti], pv[64:128, :]),
                    toks=[kv_ev[(ti - 1, "v")], kv_ev[(ti, "v")]])
                vs_done[ti] = P.op("vector", lambda e, cv=cv, ti=ti: e.tensor_copy(
                    vs[64:128, 512 * (ti - 1):512 * ti], cv[0:64, :]),
                    toks=[kv_ev[(ti, "v")]])
        t_vs9a = P.op("vector", lambda e: e.tensor_copy(
            vs[0:64, 512 * 8:512 * 9], vtmp[64:128, 0:512]),
            toks=[kv_ev[(8, "v")]])
        t_vs9b = P.op("vector", lambda e: e.memset(vs[64:128, 512 * 8:512 * 9], 0.0),
                      toks=[])
        vs_done[9] = t_vs9a

        q_ev = {}
        for w in range(4):
            wbuf = wwina if w % 2 == 0 else wwinb
            if w < 2:
                war = [kv_ev[(8, "k" if w == 0 else "v")], vs_done[9], t_vs9b]
            else:
                war = [q_ev[(w - 2, ti)] for ti in range(1, 9)]
            wtoks = load_win_q(wbuf, ag_wq_out, w, [t_ccwq] + war)
            bt = biasw if w % 2 == 0 else biasw2
            bias_war = ([kv_ev[(8, "k")]] if w == 0 else
                        [kv_ev[(8, "v")]] if w == 1 else [q_ev[(w - 2, 8)]])
            t_bq = bcast_row(bt[:, 0:512], bq_r.ap()[0:1, 512 * w:512 * (w + 1)], 512,
                             bias_war, extra_mm=bias_war)
            for ti in range(1, 9):
                slot = (ti - 1) % 3
                mm0 = wtoks + [q_ev.get((w, ti - 3)), q_ev.get((w - 1, ti)),
                               hT_done[ti]]
                mms = [P.op("tensor", lambda e, k=k, ti=ti, wb=wbuf, sl=slot: e.matmul(
                    big4[:, 512 * sl:512 * (sl + 1)],
                    hT[:, 2048 * ti + 128 * k:2048 * ti + 128 * (k + 1)],
                    wb[:, 512 * k:512 * (k + 1)],
                    start=(k == 0), stop=(k == 15)),
                    toks=mm0 if k == 0 else []) for k in range(16)]
                q_ev[(w, ti)] = P.op(
                    "vector", lambda e, ti=ti, w=w, sl=slot, bt=bt: e.tensor_add(
                        q_tok[:, 2048 * (ti - 1) + 512 * w:
                              2048 * (ti - 1) + 512 * (w + 1)],
                        big4[:, 512 * sl:512 * (sl + 1)], bt[:, 0:512]),
                    toks=mms + [t_bq])
        last_q = [q_ev[(3, ti)] for ti in range(1, 9)]
        last_q3 = q_ev[(3, 8)]

        if until == "A":
            _finish()
            return nc

        def rms_rope(tok_ap, nh, wt_ap, cbuf, sbuf_, toks):
            nhd = nh * DK
            t_sq = P.op("scalar", lambda e, n=nhd: e.activation(
                sq_scr[:, 0:n], tok_ap, AF.Square), toks=toks)
            t_ms = P.op("vector", lambda e, nh=nh, n=nhd: e.reduce_sum(
                ms16[:, 0:nh], sq_scr[:, 0:n].rearrange("p (h j) -> p h j", h=nh),
                axis=AXX), toks=[t_sq], small=True)
            t_l = P.op("scalar", lambda e, nh=nh: e.activation(
                ms16[:, 0:nh], ms16[:, 0:nh], AF.Ln, bias=epsrm_t[:, :],
                scale=1.0 / DK), toks=[t_ms, t_epsb], small=True)
            t_r = P.op("scalar", lambda e, nh=nh: e.activation(
                rs16[:, 0:nh], ms16[:, 0:nh], AF.Exp, scale=-0.5),
                toks=[t_l], small=True)
            t_n = [P.op("vector", lambda e, h=h: e.tensor_scalar(
                tok_ap[:, DK * h:DK * (h + 1)], tok_ap[:, DK * h:DK * (h + 1)],
                rs16[:, h:h + 1], None, ALU.mult),
                toks=[t_r] if h == 0 else []) for h in range(nh)]
            t_g = P.op("vector", lambda e: e.tensor_mul(tok_ap, tok_ap, wt_ap),
                       toks=t_n)
            if DEV.get("b_norope"):
                return t_g
            v3 = tok_ap.rearrange("p (h two j) -> p h two j", h=nh, two=2)
            qe = v3[:, :, 0, :]
            qo = v3[:, :, 1, :]
            cb3 = cbuf.rearrange("p (h j) -> p h j", h=nh)
            sb3 = sbuf_.rearrange("p (h j) -> p h j", h=nh)
            ms = [mscr[i][:, 0:nh * 64].rearrange("p (h j) -> p h j", h=nh)
                  for i in range(4)]
            ta = P.op("vector", lambda e: e.tensor_mul(ms[0], qe, cb3), toks=[t_g])
            tb = P.op("vector", lambda e: e.tensor_mul(ms[1], qo, sb3), toks=[])
            tc = P.op("vector", lambda e: e.tensor_mul(ms[2], qe, sb3), toks=[])
            td = P.op("vector", lambda e: e.tensor_mul(ms[3], qo, cb3), toks=[])
            te = P.op("vector", lambda e: e.tensor_sub(qe, ms[0], ms[1]),
                      toks=[ta, tb, tc, td])
            return P.op("vector", lambda e: e.tensor_add(qo, ms[2], ms[3]), toks=[te])

        qT_done = {}
        for ti in range(1, 9):
            cbuf = ropeb[(ti % 2) * 2]
            sbuf_ = ropeb[(ti % 2) * 2 + 1]
            war = [qT_done.get(ti - 2)] + (last_q if ti <= 2 else [])
            t_lc = P.dma("sync", cbuf[:, :], cosq_t.ap()[128 * ti:128 * (ti + 1), :],
                         toks=war)
            t_ls = P.dma("sync", sbuf_[:, :], sinq_t.ap()[128 * ti:128 * (ti + 1), :],
                         toks=[])
            tok_ap = q_tok[:, 2048 * (ti - 1):2048 * ti]
            tf = rms_rope(tok_ap, NH, qnt[:, :], cbuf[:, :], sbuf_[:, :],
                          [q_ev[(3, ti)], t_lc, t_ls, t_qnt])
            trs = [P.op("tensor", lambda e, k=k, ti=ti: e.transpose(
                trPS[:, 128 * k:128 * (k + 1)],
                q_tok[:, 2048 * (ti - 1) + 128 * k:2048 * (ti - 1) + 128 * (k + 1)],
                identb_s[:, :]),
                toks=[tf] + ([qT_done[ti - 1]] if (k == 0 and ti >= 2) else
                             ([last_q3] if k == 0 else []))) for k in range(16)]
            qT_done[ti] = P.op("vector", lambda e, ti=ti: e.tensor_copy(
                qT[:, 2048 * (ti - 1):2048 * ti], trPS[:, 0:2048]), toks=trs)

        kT_done = {}
        kT3 = kT[:, 0:NKV * KTW].rearrange("p (kv t) -> p kv t", kv=NKV)
        for ti in range(NT if not DEV.get("b_nok") else 0):
            cbuf = ropeb[(ti % 2) * 2]
            sbuf_ = ropeb[(ti % 2) * 2 + 1]
            war = [kT_done.get(ti - 2)] + ([qT_done[8]] if ti <= 1 else [])
            t_lc = P.dma("sync", cbuf[:, 0:NKV * 64],
                         cosk_t.ap()[128 * ti:128 * (ti + 1), :], toks=war)
            t_ls = P.dma("sync", sbuf_[:, 0:NKV * 64],
                         sink_t.ap()[128 * ti:128 * (ti + 1), :], toks=[])
            tok_ap = k_tok[:, 512 * ti:512 * (ti + 1)]
            tf = rms_rope(tok_ap, NKV, knt[:, :], cbuf[:, 0:NKV * 64],
                          sbuf_[:, 0:NKV * 64],
                          [kv_ev[(ti, "k")], t_lc, t_ls, t_knt, qT_done[8]])
            trs = [P.op("tensor", lambda e, kv=kv, ti=ti: e.transpose(
                trPS[:, 128 * kv:128 * (kv + 1)],
                k_tok[:, 512 * ti + 128 * kv:512 * ti + 128 * (kv + 1)],
                identb_s[:, :]),
                toks=[tf] + ([kT_done[ti - 1]] if (kv == 0 and ti >= 1) else
                             ([qT_done[8]] if kv == 0 else []))) for kv in range(NKV)]
            kT_done[ti] = P.op("vector", lambda e, ti=ti: e.tensor_copy(
                kT3[:, :, 128 * ti:128 * (ti + 1)],
                trPS[:, 0:512].rearrange("p (kv t) -> p kv t", kv=NKV)), toks=trs)
        t_kpad = P.op("vector", lambda e: e.memset(kT3[:, :, 1152:1216], 0.0),
                      toks=[kT_done[8]]) if not DEV.get("b_nok") else None

        if until == "B":
            _finish()
            return nc

        t_m1 = P.dma("sync", mask1_s[:, :], mask_1[:, :], toks=[kT_done[8]])
        t_mr = P.dma("sync", maskr_s[:, :], mask_r[:, :], toks=[])

        attnT_done = {}
        c_big4 = t_knt
        c_Srd = kT_done[8]
        c_Prd_t = None
        c_PTrd = None
        c_acc2 = t_knt
        c_trPS = t_kpad
        first_qk = [qT_done[8], kT_done[8], t_kpad, vs_done[9], t_vs9b, t_m1, t_mr,
                    last_q3]
        for ti in range(1, 9):
            mask_s = mask1_s if ti == 1 else maskr_s
            for g in range(2):
                w0 = 128 * ti - 64
                qkm = [P.op("tensor", lambda e, hl=hl, ti=ti, w0=w0, g=g: e.matmul(
                    big4[:, 256 * hl:256 * (hl + 1)],
                    qT[:, 2048 * (ti - 1) + 128 * (8 * g + hl):
                       2048 * (ti - 1) + 128 * (8 * g + hl + 1)],
                    kT3[:, (8 * g + hl) // NREP_C, w0:w0 + 256],
                    start=True, stop=True),
                    toks=(first_qk + [qT_done[ti], c_big4]) if hl == 0 else [])
                    for hl in range(8)]
                t_S = P.op("vector", lambda e, m=mask_s: e.tensor_add(
                    S_sb[:, :], big4[:, :], m[:, :]), toks=qkm + [c_Srd])
                c_big4 = t_S
                t_gm2 = P.op("vector", lambda e: e.reduce_max(
                    gmax[:, :], S_sb[:, :].rearrange("p (h j) -> p h j", h=8),
                    axis=AXX), toks=[t_S], small=True)
                t_ng = P.op("vector", lambda e: e.tensor_scalar(
                    ngmax[:, :], gmax[:, :], -1.0, None, ALU.mult),
                    toks=[t_gm2, c_Srd], small=True)
                t_exp = [P.op("scalar", lambda e, hl=hl: e.activation(
                    P_sb[:, 256 * hl:256 * (hl + 1)], S_sb[:, 256 * hl:256 * (hl + 1)],
                    AF.Exp, bias=ngmax[:, hl:hl + 1]),
                    toks=[t_ng, c_Prd_t] if hl == 0 else []) for hl in range(8)]
                c_Srd = t_exp[-1]
                t_gs = P.op("vector", lambda e: e.reduce_sum(
                    gsum[:, :], P_sb[:, :].rearrange("p (h j) -> p h j", h=8),
                    axis=AXX), toks=t_exp, small=True)
                t_gr = P.op("vector", lambda e: e.reciprocal(grec[:, :], gsum[:, :]),
                            toks=[t_gs], small=True)
                t_nrm = [P.op("vector", lambda e, hl=hl: e.tensor_scalar(
                    P_sb[:, 256 * hl:256 * (hl + 1)], P_sb[:, 256 * hl:256 * (hl + 1)],
                    grec[:, hl:hl + 1], None, ALU.mult),
                    toks=[t_gr] if hl == 0 else []) for hl in range(8)]
                ptt = []
                for hl in range(8):
                    ptt.append(P.op("tensor", lambda e, hl=hl: e.transpose(
                        trPS[:, 128 * hl:128 * (hl + 1)],
                        P_sb[:, 256 * hl:256 * hl + 128], identb_s[:, :]),
                        toks=[t_nrm[hl]] + ([c_trPS] if hl == 0 else [])))
                    ptt.append(P.op("tensor", lambda e, hl=hl: e.transpose(
                        trPS[:, 1024 + 128 * hl:1024 + 128 * (hl + 1)],
                        P_sb[:, 256 * hl + 128:256 * hl + 256], identb_s[:, :]),
                        toks=[]))
                c_Prd_t = ptt[-1]
                t_pt = P.op("vector", lambda e: e.tensor_copy(
                    PT_sb[:, :], trPS[:, 0:2048]), toks=ptt + [c_PTrd])
                c_trPS = t_pt
                avm = []
                for hl in range(8):
                    kv = (8 * g + hl) // NREP_C
                    avm.append(P.op("tensor", lambda e, hl=hl, kv=kv, ti=ti: e.matmul(
                        acc2[:, 128 * hl:128 * (hl + 1)],
                        vs[:, 512 * (ti - 1) + 128 * kv:512 * (ti - 1) + 128 * (kv + 1)],
                        PT_sb[:, 128 * hl:128 * (hl + 1)], start=True, stop=False),
                        toks=[t_pt, c_acc2] if hl == 0 else []))
                    avm.append(P.op("tensor", lambda e, hl=hl, kv=kv, ti=ti: e.matmul(
                        acc2[:, 128 * hl:128 * (hl + 1)],
                        vs[:, 512 * ti + 128 * kv:512 * ti + 128 * (kv + 1)],
                        PT_sb[:, 1024 + 128 * hl:1024 + 128 * (hl + 1)],
                        start=False, stop=True), toks=[]))
                c_PTrd = avm[-1]
                t_at = P.op("vector", lambda e, ti=ti, g=g: e.tensor_copy(
                    attnT[:, 2048 * (ti - 1) + 1024 * g:2048 * (ti - 1) + 1024 * (g + 1)],
                    acc2[:, 0:1024]), toks=avm)
                attnT_done[(ti, g)] = t_at
                c_acc2 = t_at
        last_attn = attnT_done[(8, 1)]

        if until == "C":
            _finish()
            return nc

        ow_done = {}
        t1_rd = {}
        for w in range(4):
            wbuf = wwoa if w % 2 == 0 else wwob
            war = ([ow_done[(w - 2, ti)] for ti in range(1, 9)] if w >= 2
                   else [last_attn])
            wtoks = load_win_q(wbuf, ag_wo_out, w, [t_ccwo] + war)
            bt = biasw if w % 2 == 0 else biasw2
            bias_war = [last_q3 if w < 2 else ow_done[(w - 2, 8)], c_big4]
            t_bo = bcast_row(bt[:, 0:512], bo_r.ap()[0:1, 512 * w:512 * (w + 1)], 512,
                             bias_war, extra_mm=bias_war)
            for ti in range(1, 9):
                slot = (ti - 1) % 3
                mm0 = wtoks + [ow_done.get((w, ti - 3)), ow_done.get((w - 1, ti)),
                               attnT_done[(ti, 1)], c_big4]
                mms = [P.op("tensor", lambda e, k=k, ti=ti, wb=wbuf, sl=slot: e.matmul(
                    big4[:, 512 * sl:512 * (sl + 1)],
                    attnT[:, 2048 * (ti - 1) + 128 * k:2048 * (ti - 1) + 128 * (k + 1)],
                    wb[:, 512 * k:512 * (k + 1)],
                    start=(k == 0), stop=(k == 15)),
                    toks=mm0 if k == 0 else []) for k in range(16)]
                par = (8 * w + ti) % 2
                t_e1 = P.op("vector", lambda e, sl=slot, bt=bt, par=par: e.tensor_add(
                    t1s[par][:, :], big4[:, 512 * sl:512 * (sl + 1)], bt[:, 0:512]),
                    toks=mms + [t_bo, t1_rd.get(par), last_attn])
                t_e2 = P.op("gpsimd", lambda e, par=par, w=w: e.tensor_mul(
                    t1s[par][:, :], t1s[par][:, :], gat_f[:, 512 * w:512 * (w + 1)]),
                    toks=[t_e1, last_attn], force_wait=True)
                t_e3 = P.op("gpsimd", lambda e, ti=ti, w=w, par=par: e.tensor_copy(
                    x2[:, 2048 * (ti - 1) + 512 * w:2048 * (ti - 1) + 512 * (w + 1)],
                    t1s[par][:, :]), toks=[t_e2])
                t1_rd[par] = t_e3
                ow_done[(w, ti)] = t_e3
        res_done = {}
        for ti in range(1, 9):
            slot = (ti - 1) % 2
            t_xr = P.dma("sync", xrel[slot][:, :], xin.ap()[128 * ti:128 * (ti + 1), :],
                         toks=[res_done.get(ti - 2), ow_done[(3, ti)]])
            res_done[ti] = P.op("vector", lambda e, ti=ti, slot=slot: e.tensor_add(
                x2[:, 2048 * (ti - 1):2048 * ti], x2[:, 2048 * (ti - 1):2048 * ti],
                xrel[slot][:, :]), toks=[t_xr, ow_done[(3, ti)]], force_wait=True)

        if until == "O":
            for ti in range(1, 9):
                P.dma("gpsimd", o_out.ap()[128 * (ti - 1):128 * ti, :],
                      x2[:, 2048 * (ti - 1):2048 * ti], toks=[res_done[ti]])
            _finish()
            return nc

        mt2_toks = build_mod_tiles(1, [res_done[8], t1_rd.get(0), t1_rd.get(1)])

        def w1_ranges(w):
            out = []
            o0 = 512 * w
            while o0 < 512 * (w + 1):
                r = o0 // W1SL
                j0 = o0 % W1SL
                ln = min(W1SL - j0, 512 * (w + 1) - o0)
                out.append((r, j0, ln, o0 - 512 * w))
                o0 += ln
            return out

        def load_w1_chunk(dst, w, up, kp, toks):
            return [P.dma(
                "sync",
                dst[:, 0:1024].rearrange("p (k j) -> p k j", k=2)
                [:, :, doff:doff + ln],
                ag_w1_out.ap().rearrange("(r k p) j -> p r k j", k=16, p=128)
                [:, r + (4 if up else 0), 2 * kp:2 * kp + 2, j0:j0 + ln],
                toks=toks) for (r, j0, ln, doff) in w1_ranges(w)]

        PSG = [big4[:, 0:512], big4[:, 1024:1536], acc2[:, 0:512], dn67[:, 0:512]]
        PSU = [big4[:, 512:1024], big4[:, 1536:2048], acc2[:, 512:1024],
               dn67[:, 512:1024]]
        DSL = [[big4[:, 512 * dw:512 * (dw + 1)] for dw in range(4)],
               [acc2[:, 0:512], acc2[:, 512:1024], dn67[:, 0:512], dn67[:, 512:1024]]]

        grp_last_out = None
        grp_gate0 = mt2_toks[-1]
        t1f_rd = None
        xqT_done = {}
        for grp in range(2):
            tis = [1 + 4 * grp + i for i in range(4)]
            for loc, ti in enumerate(tis):
                src = x2[:, 2048 * (ti - 1):2048 * ti]
                prev = ([xqT_done[tis[loc - 1]]] if loc > 0 else
                        mt2_toks + [grp_gate0])
                t_nm, t_rs = ln_chain(src, h2_s[:, :], hm2_s[:, :],
                                      [res_done[ti]] + prev, LN_EPS)
                t_xn = P.op("vector", lambda e, src=src: e.tensor_scalar(
                    h2_s[:, :], src, nmt[:, :], rstdt[:, :], ALU.add, ALU.mult),
                    toks=[t_nm, t_rs])
                t_h1 = P.op("gpsimd", lambda e: e.tensor_mul(
                    hm2_s[:, :], h2_s[:, :], modt[1][:, :]), toks=[t_xn],
                    force_wait=True)
                t_h2 = P.op("gpsimd", lambda e: e.tensor_add(
                    hm2_s[:, :], hm2_s[:, :], modt[0][:, :]), toks=[t_h1])
                t_h = P.op("vector", lambda e: e.tensor_add(
                    h2_s[:, :], hm2_s[:, :], h2_s[:, :]), toks=[t_h2],
                    force_wait=True)
                if until == "F0" and grp == 0 and loc == 0:
                    P.dma("gpsimd", o_out.ap()[0:128, :], h2_s[:, :], toks=[t_h])
                    P.dma("gpsimd", o_out.ap()[128:256, :],
                          x2[:, 0:2048], toks=[t_h])
                    P.dma("gpsimd", o_out.ap()[256:384, :], hm2_s[:, :], toks=[t_h])
                    _finish()
                    return nc
                t_mx = P.op("vector", lambda e: e.reduce_max(
                    mx1[:, :], h2_s[:, :], axis=AXX, apply_absolute_value=True),
                    toks=[t_h], small=True)
                t_sc = P.op("vector", lambda e: e.tensor_scalar(
                    sc1[:, :], mx1[:, :], 1.0 / QMAX, float(Q_EPS), ALU.mult, ALU.max),
                    toks=[t_mx], small=True)
                t_rc = P.op("vector", lambda e: e.reciprocal(rc1[:, :], sc1[:, :]),
                            toks=[t_sc], small=True)
                t_ca = P.op("scalar", lambda e: e.activation(
                    rc1a[:, :], rc1[:, :], AF.Identity), toks=[t_rc], small=True)
                t_cb = P.op("scalar", lambda e: e.activation(
                    sc1a[:, :], sc1[:, :], AF.Identity), toks=[t_rc], small=True)
                t_r1 = P.op("vector", lambda e: e.tensor_scalar(
                    h2_s[:, :], h2_s[:, :], rc1a[:, :], MAGIC, ALU.mult, ALU.add),
                    toks=[t_ca])
                t_r2 = P.op("vector", lambda e: e.tensor_scalar(
                    hm2_s[:, :], h2_s[:, :], -MAGIC, sc1a[:, :], ALU.add, ALU.mult),
                    toks=[t_r1, t_cb], force_wait=True)
                trs = [P.op("tensor", lambda e, k=k, loc=loc: e.transpose(
                    trPS[:, 128 * k:128 * (k + 1)], hm2_s[:, 128 * k:128 * (k + 1)],
                    identb_s[:, :]),
                    toks=[t_r2] + ([xqT_done.get(tis[loc - 1]) if loc > 0
                                    else grp_gate0] if k == 0 else []))
                    for k in range(16)]
                xqT_done[ti] = P.op("vector", lambda e, loc=loc: e.tensor_copy(
                    xqT[:, 2048 * loc:2048 * (loc + 1)], trPS[:, 0:2048]), toks=trs)
                if until == "F1" and grp == 0 and loc == 0:
                    P.dma("gpsimd", o_out.ap()[128:256, :], hm2_s[:, :],
                          toks=[t_r2, xqT_done[ti]])
                    P.dma("gpsimd", o_out.ap()[256:384, :], h2_s[:, :],
                          toks=[t_r2, xqT_done[ti]])

            hh_ev = {}
            mm_lastk = {}
            for w in range(11):
                for kp in range(8):
                    sg_ = w1ring[kp % 2]
                    su_ = w1ring[2 + kp % 2]
                    war = [mm_lastk.get((w, kp - 2)) if kp >= 2 else
                           mm_lastk.get((w - 1, kp + 6))]
                    if w == 0 and kp < 2:
                        war = [mt2_toks[-1], grp_last_out]
                    lg = load_w1_chunk(sg_, w, False, kp, war)
                    lu = load_w1_chunk(su_, w, True, kp, [])
                    if until == "F4" and grp == 0 and w == 0 and kp == 0:
                        P.dma("gpsimd", o_out.ap()[0:128, 0:1024], sg_[:, :],
                              toks=lg + lu)
                        P.dma("gpsimd", o_out.ap()[0:128, 1024:2048], su_[:, :],
                              toks=lg + lu)
                        _finish()
                        return nc
                    for loc in range(4):
                        ti = tis[loc]
                        t0 = lg + lu
                        if kp == 0:
                            t0 = t0 + [xqT_done[ti], hh_ev.get((w - 1, loc)),
                                       grp_gate0 if w == 0 else None]
                        P.op("tensor", lambda e, kp=kp, loc=loc, sg=sg_: e.matmul(
                            PSG[loc], xqT[:, 2048 * loc + 256 * kp:
                                          2048 * loc + 256 * kp + 128],
                            sg[:, 0:512], start=(kp == 0), stop=False),
                            toks=t0 if (kp == 0 or loc == 0) else [])
                        P.op("tensor", lambda e, kp=kp, loc=loc, sg=sg_: e.matmul(
                            PSG[loc], xqT[:, 2048 * loc + 256 * kp + 128:
                                          2048 * loc + 256 * (kp + 1)],
                            sg[:, 512:1024], start=False, stop=(kp == 7)), toks=[])
                        P.op("tensor", lambda e, kp=kp, loc=loc, su=su_: e.matmul(
                            PSU[loc], xqT[:, 2048 * loc + 256 * kp:
                                          2048 * loc + 256 * kp + 128],
                            su[:, 0:512], start=(kp == 0), stop=False), toks=[])
                        mm_lastk[(w, kp)] = P.op(
                            "tensor", lambda e, kp=kp, loc=loc, su=su_: e.matmul(
                                PSU[loc], xqT[:, 2048 * loc + 256 * kp + 128:
                                              2048 * loc + 256 * (kp + 1)],
                                su[:, 512:1024], start=False, stop=(kp == 7)),
                            toks=[])
                if until == "F3" and grp == 0 and w == 0:
                    t_dbg = P.op("vector", lambda e: e.tensor_copy(
                        rscr[:, :], PSG[0]), toks=[mm_lastk[(w, 7)]])
                    P.dma("gpsimd", o_out.ap()[0:128, 0:512], rscr[:, :],
                          toks=[t_dbg])
                    t_dbg2 = P.op("vector", lambda e: e.tensor_copy(
                        rscr[:, :], PSU[0]), toks=[t_dbg], force_wait=True)
                    P.dma("gpsimd", o_out.ap()[0:128, 512:1024], rscr[:, :],
                          toks=[t_dbg2])
                    _finish()
                    return nc
                for loc in range(4):
                    sg_ap = sgl4[:, 512 * loc:512 * (loc + 1)]
                    t_si = P.op("scalar", lambda e, loc=loc, sa=sg_ap: e.activation(
                        sa, PSG[loc], AF.Silu),
                        toks=[mm_lastk[(w, 7)], hh_ev.get((w - 1, loc))])
                    hh_ev[(w, loc)] = P.op("vector",
                                           lambda e, loc=loc, w=w, sa=sg_ap: e.tensor_mul(
                        hbuf[:, I_FF * loc + 512 * w:I_FF * loc + 512 * (w + 1)],
                        sa, PSU[loc]), toks=[t_si])

            all_hh = [hh_ev[(10, loc)] for loc in range(4)]
            if until == "F1" and grp == 0:
                P.dma("gpsimd", o_out.ap()[896:1024, 0:2048], xqT[:, 0:2048],
                      toks=all_hh)
                P.dma("gpsimd", o_out.ap()[0:128, 0:2048], hbuf[:, 0:2048],
                      toks=all_hh)
                P.dma("gpsimd", o_out.ap()[384:512, 0:2048],
                      hbuf[:, 2048:4096], toks=all_hh)
                P.dma("gpsimd", o_out.ap()[512:640, 0:2048],
                      modt[0][:, :], toks=all_hh)
                P.dma("gpsimd", o_out.ap()[640:768, 0:2048],
                      modt[1][:, :], toks=all_hh)
                P.dma("gpsimd", o_out.ap()[768:896, 0:2048],
                      gat_f[:, :], toks=all_hh)
                _finish()
                return nc
            pair_last_f1 = None
            for pair in range(2):
                locs = (2 * pair, 2 * pair + 1)
                hq_done = {}
                for li, loc in enumerate(locs):
                    t_mx = P.op("vector", lambda e, loc=loc: e.reduce_max(
                        mx1[:, :], hbuf[:, I_FF * loc:I_FF * (loc + 1)], axis=AXX,
                        apply_absolute_value=True),
                        toks=all_hh + [t1f_rd], small=True)
                    t_sc = P.op("vector", lambda e: e.tensor_scalar(
                        sc1[:, :], mx1[:, :], 1.0 / QMAX, float(Q_EPS),
                        ALU.mult, ALU.max), toks=[t_mx], small=True)
                    rtok = P.op("vector", lambda e: e.reciprocal(rc1[:, :], sc1[:, :]),
                                toks=[t_sc], small=True)
                    t_ca = P.op("scalar", lambda e: e.activation(
                        rc1a[:, :], rc1[:, :], AF.Identity), toks=[rtok], small=True)
                    t_cb = P.op("scalar", lambda e: e.activation(
                        sc1a[:, :], sc1[:, :], AF.Identity), toks=[rtok], small=True)
                    rtok = t_cb
                    for cw in range(11):
                        ha = hbuf[:, I_FF * loc + 512 * cw:I_FF * loc + 512 * (cw + 1)]
                        t_a = P.op("vector", lambda e, ha=ha: e.tensor_scalar(
                            rscr[:, :], ha, rc1a[:, :], MAGIC, ALU.mult, ALU.add),
                            toks=[rtok, t_ca])
                        rtok = P.op("vector", lambda e, ha=ha: e.tensor_scalar(
                            ha, rscr[:, :], -MAGIC, sc1a[:, :], ALU.add, ALU.mult),
                            toks=[t_a], force_wait=True)
                    tr_ev = []
                    for gi in range(3):
                        k0 = 16 * gi
                        n = min(16, 44 - k0)
                        tk = [rtok] if gi == 0 else [tr_ev[gi - 1]]
                        if gi == 0 and pair_last_f1 is not None:
                            tk = tk + [pair_last_f1]
                        trg = [P.op("tensor",
                                    lambda e, kk=k0 + j, p=j, loc=loc:
                                    e.transpose(
                            trPS[:, 128 * p:128 * (p + 1)],
                            hbuf[:, I_FF * loc + 128 * kk:I_FF * loc + 128 * (kk + 1)],
                            identb_s[:, :]), toks=tk if j == 0 else [])
                            for j in range(n)]
                        tr_ev.append(P.op(
                            "vector", lambda e, k0=k0, n=n, li=li: e.tensor_copy(
                                hqT[li][:, 128 * k0:128 * (k0 + n)],
                                trPS[:, 0:128 * n]), toks=trg))
                    hq_done[li] = tr_ev[-1]
                dmm = {}
                w2_tok = [None, None]
                for k in range(44):
                    ring = w2ring[k % 2]
                    war = [dmm.get((k - 2, 1, 3))]
                    if k < 2:
                        war = war + [mt2_toks[-1]]
                    w2_tok[k % 2] = P.dma(
                        "sync", ring[:, :].rearrange("p (r j) -> p r j", r=8),
                        bass.AP(ag_w2_out, W2SL * 128 * k,
                                [[W2SL, 128], [I_FF * W2SL, 8], [1, W2SL]]),
                        toks=[t_ccw2] + war)
                    for li in range(2):
                        for dw in range(4):
                            t0 = []
                            if li == 0 and dw == 0:
                                t0 = [w2_tok[k % 2]]
                                if k == 0:
                                    t0 += [hq_done[0], hq_done[1]]
                            dmm[(k, li, dw)] = P.op(
                                "tensor", lambda e, k=k, li=li, dw=dw, r=ring: e.matmul(
                                    DSL[li][dw], hqT[li][:, 128 * k:128 * (k + 1)],
                                    r[:, 512 * dw:512 * (dw + 1)],
                                    start=(k == 0), stop=(k == 43)), toks=t0)
                for li, loc in enumerate(locs):
                    ti = tis[loc]
                    fin = []
                    for dw in range(4):
                        t_f1 = P.op("vector", lambda e, li=li, dw=dw: e.tensor_mul(
                            t1f[:, :], DSL[li][dw],
                            gat_f[:, 512 * dw:512 * (dw + 1)]),
                            toks=[dmm[(43, li, dw)], t1f_rd])
                        t_f2 = P.op("gpsimd", lambda e, ti=ti, dw=dw: e.tensor_add(
                            x2[:, 2048 * (ti - 1) + 512 * dw:
                               2048 * (ti - 1) + 512 * (dw + 1)],
                            x2[:, 2048 * (ti - 1) + 512 * dw:
                               2048 * (ti - 1) + 512 * (dw + 1)],
                            t1f[:, :]), toks=[t_f1], force_wait=True)
                        t1f_rd = t_f2
                        fin.append(t_f2)
                        pair_last_f1 = t_f1
                    grp_last_out = P.dma("gpsimd",
                                         o_out.ap()[128 * (ti - 1):128 * ti, :],
                                         x2[:, 2048 * (ti - 1):2048 * ti], toks=fin)
            grp_gate0 = pair_last_f1

        _finish()
    return nc


NREP_C = NH // NKV


_NC_CACHE = {}


def _get_nc():
    if "nc" not in _NC_CACHE:
        _NC_CACHE["nc"] = build_nc()
    return _NC_CACHE["nc"]


def kernel(L0_attn_x, L0_attn_Wq, L0_attn_bq, L0_attn_Wk, L0_attn_bk, L0_attn_Wv,
           L0_attn_bv, L0_attn_Wo, L0_attn_bo, L0_attn_qn_w, L0_attn_kn_w,
           L0_attn_scale, L0_ffn_w_gate_up, L0_ffn_w_down, L0_c, L0_wmod):
    x = np.ascontiguousarray(np.asarray(L0_attn_x, dtype=np.float32))
    permq = _perm_full(NH)
    permk = _perm_full(NKV)
    permh = _perm_head()
    wq_p = np.ascontiguousarray(np.asarray(L0_attn_Wq, np.float32)[permq, :])
    wk_p = np.ascontiguousarray(np.asarray(L0_attn_Wk, np.float32)[permk, :])
    wv_p = np.ascontiguousarray(np.asarray(L0_attn_Wv, np.float32))
    wo_p = np.ascontiguousarray(np.asarray(L0_attn_Wo, np.float32))
    bq_p = np.ascontiguousarray(np.asarray(L0_attn_bq, np.float32)[permq][None, :])
    bk_p = np.ascontiguousarray(np.asarray(L0_attn_bk, np.float32)[permk][None, :])
    bv_p = np.ascontiguousarray(np.asarray(L0_attn_bv, np.float32)[None, :])
    bo_p = np.ascontiguousarray(np.asarray(L0_attn_bo, np.float32)[None, :])
    qn_p = np.ascontiguousarray(
        np.tile(np.asarray(L0_attn_qn_w, np.float32)[permh], NH)[None, :])
    kn_p = np.ascontiguousarray(
        np.tile(np.asarray(L0_attn_kn_w, np.float32)[permh], NKV)[None, :])
    scal = np.asarray(L0_attn_scale, np.float32).reshape(1, 1)
    w1 = np.ascontiguousarray(np.asarray(L0_ffn_w_gate_up, np.float32))
    w2 = np.ascontiguousarray(np.asarray(L0_ffn_w_down, np.float32))
    cfull = np.ascontiguousarray(np.asarray(L0_c, np.float32))
    wmod = np.ascontiguousarray(np.asarray(L0_wmod, np.float32))

    identb = np.eye(128, dtype=ml_dtypes.bfloat16)
    identf = np.eye(128, dtype=np.float32)
    onesr = np.ones((1, 128), dtype=np.float32)
    m1h = _mask_tile(True)
    mr = _mask_tile(False)

    in_maps = []
    for c in range(NCORE):
        b, half = c // 2, c % 2
        if half == 0:
            xs = np.concatenate([x[b, 0:128], x[b, 0:1024]], axis=0)
            s0 = -128
            mask1 = m1h
        else:
            xs = x[b, 896:2048]
            s0 = 896
            mask1 = mr
        cq, sq, ck, sk = _rope_tables(s0)
        selm = np.zeros((32, 8), dtype=np.float32)
        for r in range(8):
            selm[4 * r + b, r] = 1.0
        in_maps.append({
            "xin": np.ascontiguousarray(xs),
            "cfull": cfull,
            "wmod_sl": np.ascontiguousarray(wmod[:, MODSL * c:MODSL * (c + 1)]),
            "w1_sl": np.ascontiguousarray(w1[W1SL * c:W1SL * (c + 1), :]),
            "w2_sl": np.ascontiguousarray(w2[W2SL * c:W2SL * (c + 1), :]),
            "wq_sl": np.ascontiguousarray(wq_p[WQSL * c:WQSL * (c + 1), :]),
            "wk_sl": np.ascontiguousarray(wk_p[WKSL * c:WKSL * (c + 1), :]),
            "wv_sl": np.ascontiguousarray(wv_p[WKSL * c:WKSL * (c + 1), :]),
            "wo_sl": np.ascontiguousarray(wo_p[WQSL * c:WQSL * (c + 1), :]),
            "bq_r": bq_p, "bk_r": bk_p, "bv_r": bv_p, "bo_r": bo_p,
            "qn16": qn_p, "kn4": kn_p, "scal": scal, "sel": selm,
            "identb": identb, "identf": identf, "onesr": onesr,
            "cosq_t": cq, "sinq_t": sq, "cosk_t": ck, "sink_t": sk,
            "mask_1": mask1, "mask_r": mr,
        })

    nc = _get_nc()
    res = run_bass_kernel_spmd(nc, in_maps, core_ids=list(range(NCORE)))
    out = np.empty((B, S, D), dtype=np.float32)
    for c in range(NCORE):
        b, half = c // 2, c % 2
        out[b, 1024 * half:1024 * (half + 1), :] = np.asarray(res.results[c]["o_out"])
    return out


# version 07 (final)
# speedup vs baseline: 1.0721x; 1.0721x over previous
import numpy as np
import ml_dtypes
import concourse.bass as bass
import concourse.mybir as mybir
from concourse.bass_utils import run_bass_kernel_spmd
from contextlib import ExitStack

F32 = mybir.dt.float32
BF16 = mybir.dt.bfloat16
AF = mybir.ActivationFunctionType
ALU = mybir.AluOpType
AXX = mybir.AxisListType.X

D = 2048
DK = 128
NH = 16
NKV = 4
S = 2048
B = 4
I_FF = 5632
NT = 9
TLOC = NT * 128
TMAIN = 1024
NCORE = 8
W1SL = 1408
W2SL = 256
WQSL = 256
WKSL = 64
MODSL = 1536
MAGIC = float(1.5 * 2 ** 23)
MASKF = -1.0e9
LN_EPS = 1e-5
RMS_EPS = 1e-8
Q_EPS = 1e-8
QMAX = 127.0
KTW = 1216
KB = 1024
DEV = {}


def _perm_head(dk=DK):
    return np.concatenate([np.arange(0, dk, 2), np.arange(1, dk, 2)])


def _perm_full(nheads, dk=DK):
    p = _perm_head(dk)
    return np.concatenate([h * dk + p for h in range(nheads)])


def _rope_tables(s0):
    inv = 1.0 / (10000.0 ** (np.arange(0, DK, 2, dtype=np.float64) / DK))
    pos = np.maximum(s0 + np.arange(TLOC, dtype=np.float64), 0.0)
    ph = pos[:, None] * inv[None, :]
    cos = np.cos(ph).astype(np.float32)
    sin = np.sin(ph).astype(np.float32)
    return (np.ascontiguousarray(np.tile(cos, (1, NH))),
            np.ascontiguousarray(np.tile(sin, (1, NH))),
            np.ascontiguousarray(np.tile(cos, (1, NKV))),
            np.ascontiguousarray(np.tile(sin, (1, NKV))))


def _mask_tile(first_kill_halo):
    i = np.arange(128)[:, None]
    k = np.arange(256)[None, :]
    ok = (k >= i) & (k <= i + 64) & (k < 192)
    if first_kill_halo:
        ok = ok & (k >= 64)
    one = np.where(ok, np.float32(0.0), np.float32(MASKF)).astype(np.float32)
    return np.ascontiguousarray(np.tile(one, (1, 8)))


class Sem:
    def __init__(self, es, nc, name):
        self.h = es.enter_context(nc.semaphore(name))
        self.n = 0
        self.name = name


class Plan:
    ENGS = ("sync", "vector", "scalar", "tensor", "gpsimd")

    def __init__(self, es, nc):
        self.nc = nc
        self.plan = {e: [] for e in self.ENGS}
        self.sem = {e: Sem(es, nc, "s_" + e) for e in self.ENGS}
        self.counts = {e: 0 for e in self.ENGS}

    def _waits(self, eng, toks, force=False):
        best = {}
        for t in toks:
            if t is None:
                continue
            sem, v, peng, small = t
            if (not force) and peng == eng and not small:
                continue
            cur = best.get(id(sem))
            if cur is None or cur[1] < v:
                best[id(sem)] = (sem, v)
        for sem, v in best.values():
            assert v <= sem.n, f"wait on future {sem.name} {v}>{sem.n}"
            self.plan[eng].append(lambda e, s=sem, vv=v: e.wait_ge(s.h, vv))

    def op(self, eng, fn, toks=(), small=False, force_wait=False):
        self._waits(eng, toks, force=force_wait)
        sem = self.sem[eng]
        sem.n += 1
        self.plan[eng].append(lambda e, f=fn, s=sem: f(e).then_inc(s.h, 1))
        self.counts[eng] += 1
        return (sem, sem.n, eng, small)

    def dma(self, queue, out, in_, toks=(), **kw):
        self._waits(queue, toks, force=True)
        sem = self.sem[queue]
        sem.n += 16
        self.plan[queue].append(
            lambda e, o=out, i=in_, s=sem, k=kw: e.dma_start(out=o, in_=i, **k)
            .then_inc(s.h, 16))
        self.counts[queue] += 1
        return (sem, sem.n, queue, False)

    def raw(self, eng, fn):
        self.plan[eng].append(fn)


def build_nc(tap=None, until=None):
    nc = bass.Bass("TRN2", target_bir_lowering=False)
    dt = nc.dram_tensor

    xin = dt("xin", [TLOC, D], F32, kind="ExternalInput")
    cfull = dt("cfull", [B, D], F32, kind="ExternalInput")
    wmod_sl = dt("wmod_sl", [D, MODSL], F32, kind="ExternalInput")
    w1_sl = dt("w1_sl", [W1SL, D], F32, kind="ExternalInput")
    w2_sl = dt("w2_sl", [W2SL, I_FF], F32, kind="ExternalInput")
    wq_sl = dt("wq_sl", [WQSL, D], F32, kind="ExternalInput")
    wk_sl = dt("wk_sl", [WKSL, D], F32, kind="ExternalInput")
    wv_sl = dt("wv_sl", [WKSL, D], F32, kind="ExternalInput")
    wo_sl = dt("wo_sl", [WQSL, D], F32, kind="ExternalInput")
    bq_r = dt("bq_r", [1, D], F32, kind="ExternalInput")
    bk_r = dt("bk_r", [1, NKV * DK], F32, kind="ExternalInput")
    bv_r = dt("bv_r", [1, NKV * DK], F32, kind="ExternalInput")
    bo_r = dt("bo_r", [1, D], F32, kind="ExternalInput")
    qn16 = dt("qn16", [1, D], F32, kind="ExternalInput")
    kn4 = dt("kn4", [1, NKV * DK], F32, kind="ExternalInput")
    scal = dt("scal", [1, 1], F32, kind="ExternalInput")
    sel = dt("sel", [32, 8], F32, kind="ExternalInput")
    identb = dt("identb", [128, 128], BF16, kind="ExternalInput")
    identf = dt("identf", [128, 128], F32, kind="ExternalInput")
    onesr = dt("onesr", [1, 128], F32, kind="ExternalInput")
    cosq_t = dt("cosq_t", [TLOC, NH * 64], F32, kind="ExternalInput")
    sinq_t = dt("sinq_t", [TLOC, NH * 64], F32, kind="ExternalInput")
    cosk_t = dt("cosk_t", [TLOC, NKV * 64], F32, kind="ExternalInput")
    sink_t = dt("sink_t", [TLOC, NKV * 64], F32, kind="ExternalInput")
    mask_1 = dt("mask_1", [128, 8 * 256], F32, kind="ExternalInput")
    mask_r = dt("mask_r", [128, 8 * 256], F32, kind="ExternalInput")

    o_out = dt("o_out", [TMAIN, D], F32, kind="ExternalOutput")

    ag_mod_in = dt("ag_mod_in", [B, MODSL], F32)
    ag_mod_out = dt("ag_mod_out", [NCORE * B, MODSL], F32, addr_space="Shared")
    ag_w1_in = dt("ag_w1_in", [D, W1SL], BF16)
    ag_w1_out = dt("ag_w1_out", [NCORE * D, W1SL], BF16, addr_space="Shared")
    ag_w2_in = dt("ag_w2_in", [1, I_FF * W2SL], BF16)
    ag_w2_out = dt("ag_w2_out", [NCORE, I_FF * W2SL], BF16, addr_space="Shared")
    ag_wq_in = dt("ag_wq_in", [D, WQSL], BF16)
    ag_wq_out = dt("ag_wq_out", [NCORE * D, WQSL], BF16, addr_space="Shared")
    ag_wk_in = dt("ag_wk_in", [D, WKSL], BF16)
    ag_wk_out = dt("ag_wk_out", [NCORE * D, WKSL], BF16, addr_space="Shared")
    ag_wv_in = dt("ag_wv_in", [D, WKSL], BF16)
    ag_wv_out = dt("ag_wv_out", [NCORE * D, WKSL], BF16, addr_space="Shared")
    ag_wo_in = dt("ag_wo_in", [D, WQSL], BF16)
    ag_wo_out = dt("ag_wo_out", [NCORE * D, WQSL], BF16, addr_space="Shared")
    modrow_d = dt("modrow_d", [1, 6 * D], F32)

    es = ExitStack()
    with es:
        trPS = es.enter_context(nc.psum_tensor("trPS", [128, 2048], BF16))
        big4 = es.enter_context(nc.psum_tensor("big4", [128, 2048], F32))
        acc2 = es.enter_context(nc.psum_tensor("acc2", [128, 1024], F32))
        dn67 = trPS[:, 0:2048].bitcast(F32)

        sbuf_base = (nc._sbuf_addr_for_side("left") + 1023) & ~1023
        sbuf_cap = nc.sbuf_bytes_remaining - 2048
        uid = [0]

        def mk(name, shape, dtype, off_b):
            uid[0] += 1
            per = int(np.prod(shape[1:])) * mybir.dt.size(dtype)
            assert off_b + per <= sbuf_cap, f"{name}: {off_b}+{per} > {sbuf_cap}"
            return nc.alloc_sbuf_tensor_at(f"{name}_{uid[0]}", list(shape), dtype,
                                           offset=sbuf_base + off_b)

        P = Plan(es, nc)
        sCC = [Sem(es, nc, f"sCC{i}") for i in range(7)]

        def _finish():
            P.raw("gpsimd", lambda e: e.wait_ge(P.sem["gpsimd"].h, P.sem["gpsimd"].n))
            P.raw("sync", lambda e: e.wait_ge(P.sem["sync"].h, P.sem["sync"].n))
            with nc.Block() as block:
                for eng in Plan.ENGS:
                    getattr(block, eng)(lambda E, eng=eng: [f(E) for f in P.plan[eng]])
            build_nc.last_stats = {"counts": dict(P.counts),
                                   "sems": {e: P.sem[e].n for e in Plan.ENGS}}

        def cc(idx, ag_in, ag_out, toks):
            P._waits("gpsimd", toks, force=True)
            s = sCC[idx]
            s.n += 1
            P.plan["gpsimd"].append(
                lambda e, a=ag_in, b=ag_out, ss=s: e.collective_compute(
                    "AllGather", ALU.bypass, replica_groups=[list(range(NCORE))],
                    ins=[a.ap().opt()], outs=[b.ap().opt()]).then_inc(ss.h, 1))
            return (s, s.n, "gpsimd", False)

        poff = [0]

        def pk(name, shape, dtype):
            h = mk(name, shape, dtype, poff[0])
            per = int(np.prod(shape[1:])) * mybir.dt.size(dtype)
            poff[0] += (per + 31) & ~31
            return h

        identb_s = pk("identb", [128, 128], BF16)
        identf_s = pk("identf", [128, 128], F32)
        onesr_s = pk("onesr", [1, 128], F32)
        modt = [pk(f"modt{i}", [128, D], BF16) for i in range(2)]
        gat_f = pk("gat_f", [128, D], F32)
        s1t = pk("s1t", [128, 1], F32)
        s2t = pk("s2t", [128, 1], F32)
        mt = pk("mt", [128, 1], F32)
        nmt = pk("nmt", [128, 1], F32)
        vart = pk("vart", [128, 1], F32)
        rstdt = pk("rstdt", [128, 1], F32)
        ms16 = pk("ms16", [128, 16], F32)
        rs16 = pk("rs16", [128, 16], F32)
        gmax = pk("gmax", [128, 8], F32)
        ngmax = pk("ngmax", [128, 8], F32)
        gsum = pk("gsum", [128, 8], F32)
        grec = pk("grec", [128, 8], F32)
        mx1 = pk("mx1", [128, 1], F32)
        sc1 = pk("sc1", [128, 1], F32)
        rc1 = pk("rc1", [128, 1], F32)
        sc1a = pk("sc1a", [128, 1], F32)
        rc1a = pk("rc1a", [128, 1], F32)
        scalb = pk("scalb", [128, 1], F32)
        row1 = pk("row1", [1, 512], F32)
        tiny2 = pk("tiny2", [1, 512], F32)
        tinyb = pk("tinyb", [1, 512], F32)
        epsln_t = pk("epsln_t", [128, 1], F32)
        epsrm_t = pk("epsrm_t", [128, 1], F32)
        R0 = (poff[0] + 1023) & ~1023

        def rg(name, shape, dtype, off_kb):
            return mk(name, shape, dtype, R0 + int(off_kb * KB))

        t_epsa = P.op("vector", lambda e: e.memset(epsln_t[:, :], float(LN_EPS)),
                      small=True)
        t_epsb = P.op("vector", lambda e: e.memset(epsrm_t[:, :], float(RMS_EPS)),
                      small=True)
        t_idb = P.dma("sync", identb_s[:, :], identb[:, :])
        t_idf = P.dma("sync", identf_s[:, :], identf[:, :])
        t_one = P.dma("sync", onesr_s[:, :], onesr[:, :])

        def bcast_row(dst_ap, src_row_ap, width, toks, extra_mm=()):
            t_ld = P.dma("sync", row1[0:1, 0:width], src_row_ap, toks=toks)
            mm = P.op("tensor", lambda e, w=width: e.matmul(
                big4[:, 1536:1536 + w], onesr_s[0:1, :], row1[0:1, 0:w],
                start=True, stop=True), toks=[t_ld, t_one] + list(extra_mm))
            return P.op("vector", lambda e, w=width: e.tensor_copy(
                dst_ap, big4[:, 1536:1536 + w]), toks=[mm])

        def ln_chain(src_ap, dump_f32, dump_b16, toks, eps):
            t_s1 = P.op("vector", lambda e: e.tensor_scalar(
                dump_f32, src_ap, 1.0, 0.0, ALU.mult, ALU.add,
                accum_out=s1t[:, :]), toks=toks, small=True)
            t_s2 = P.op("scalar", lambda e: e.activation(
                dump_b16, src_ap, AF.Square, accum_out=s2t[:, :]),
                toks=toks, small=True)
            t_m = P.op("vector", lambda e: e.tensor_scalar(
                mt[:, :], s1t[:, :], 1.0 / D, None, ALU.mult),
                toks=[t_s1, t_s2], small=True)
            t_nm = P.op("vector", lambda e: e.tensor_scalar(
                nmt[:, :], mt[:, :], -1.0, None, ALU.mult), toks=[t_m], small=True)
            t_v1 = P.op("vector", lambda e: e.scalar_tensor_tensor(
                vart[:, :], mt[:, :], -1.0, mt[:, :], ALU.mult, ALU.mult),
                toks=[t_nm], small=True)
            t_v2 = P.op("vector", lambda e: e.tensor_scalar(
                s2t[:, :], s2t[:, :], 1.0 / D, None, ALU.mult), toks=[t_v1], small=True)
            t_v3 = P.op("vector", lambda e: e.tensor_add(
                vart[:, :], vart[:, :], s2t[:, :]), toks=[t_v2], small=True)
            t_ln = P.op("scalar", lambda e: e.activation(
                vart[:, :], vart[:, :], AF.Ln, bias=epsln_t[:, :]),
                toks=[t_v3, t_epsa], small=True)
            t_rs = P.op("scalar", lambda e: e.activation(
                rstdt[:, :], vart[:, :], AF.Exp, scale=-0.5), toks=[t_ln], small=True)
            return t_nm, t_rs

        csb = rg("csb", [B, D], F32, 0)
        silc = rg("silc", [B, D], F32, 8)
        silcT = rg("silcT", [128, 64], F32, 16)
        wmodb = [rg(f"wmodb{i}", [128, MODSL], F32, 16.5 + 6 * i) for i in range(2)]
        gmod_s = rg("gmod_s", [32, MODSL], F32, 28.5)
        selb = rg("selb", [32, 8], F32, 34.5)
        mod_sb = rg("mod_sb", [B, MODSL], F32, 35)
        prep_bf = rg("prep_bf", [128, I_FF], BF16, 41)
        stag = rg("stag", [128, 44 * 128], BF16, 52.5)
        prep_in = [rg(f"prep_in{i}", [128, I_FF], F32, 127 + 22 * i) for i in range(2)]

        t_c = P.dma("sync", csb[:, :], cfull[:, :])
        t_sel = P.dma("sync", selb[:, :], sel[:, :])

        t_sil = P.op("scalar", lambda e: e.activation(silc[:, :], csb[:, :], AF.Silu),
                     toks=[t_c])
        trt = [P.op("tensor", lambda e, k=k: e.transpose(
            acc2[:, 4 * k:4 * k + 4], silc[0:4, 128 * k:128 * (k + 1)], identf_s[0:4, 0:4]),
            toks=[t_sil, t_idf]) for k in range(16)]
        t_silT = P.op("vector", lambda e: e.tensor_copy(silcT[:, :], acc2[:, 0:64]),
                      toks=trt)

        mod_mms = []
        wm_tok = [None, None]
        for k in range(16):
            buf = wmodb[k % 2]
            war = mod_mms[3 * (k - 2) + 2] if k >= 2 else None
            wm_tok[k % 2] = P.dma("sync", buf[:, :],
                                  wmod_sl.ap()[128 * k:128 * (k + 1), :], toks=[war])
            for u in range(3):
                t = P.op("tensor", lambda e, k=k, u=u, b=buf: e.matmul(
                    big4[0:4, 512 * u:512 * (u + 1)], silcT[:, 4 * k:4 * k + 4],
                    b[:, 512 * u:512 * (u + 1)], start=(k == 0), stop=(k == 15)),
                    toks=[wm_tok[k % 2], t_silT] if u == 0 else [])
                mod_mms.append(t)
        t_mev = P.op("vector", lambda e: e.tensor_copy(mod_sb[:, :], big4[0:4, 0:1536]),
                     toks=[mod_mms[-1]])
        t_mw = P.dma("gpsimd", ag_mod_in.ap(), mod_sb[:, :], toks=[t_mev])
        t_ccmod = cc(0, ag_mod_in, ag_mod_out, [t_mw])
        if until == "p1":
            _finish()
            return nc

        def prep_transpose(src_dram, nrows, ag_in):
            outs = []
            for oc in range((nrows + 127) // 128):
                rr = min(128, nrows - 128 * oc)
                t_ld = P.dma("gpsimd", prep_bf[0:rr, 0:D],
                             src_dram.ap()[128 * oc:128 * oc + rr, :],
                             toks=outs[-1:])
                trs = [P.op("tensor", lambda e, k=k, rr=rr: e.transpose(
                    trPS[:, 128 * k:128 * k + rr], prep_bf[0:rr, 128 * k:128 * (k + 1)],
                    identb_s[0:rr, 0:rr]), toks=[t_ld, t_idb]) for k in range(16)]
                t_ev = P.op("vector", lambda e, rr=rr: e.tensor_copy(
                    stag[:, 0:16 * rr].rearrange("p (k j) -> p k j", k=16),
                    trPS[:, 0:2048].rearrange("p (k j) -> p k j", k=16)[:, :, 0:rr]),
                    toks=trs)
                t_st = P.dma("gpsimd",
                             ag_in.ap().rearrange("(k p) j -> p k j", p=128)
                             [:, :, 128 * oc:128 * oc + rr],
                             stag[:, 0:16 * rr].rearrange("p (k j) -> p k j", k=16),
                             toks=[t_ev])
                outs.append(t_st)
            return outs

        t_ccwq = cc(1, ag_wq_in, ag_wq_out, prep_transpose(wq_sl, WQSL, ag_wq_in))
        t_ccwk = cc(2, ag_wk_in, ag_wk_out, prep_transpose(wk_sl, WKSL, ag_wk_in))
        t_ccwv = cc(3, ag_wv_in, ag_wv_out, prep_transpose(wv_sl, WKSL, ag_wv_in))
        t_ccwo = cc(4, ag_wo_in, ag_wo_out, prep_transpose(wo_sl, WQSL, ag_wo_in))
        if until == "p2":
            _finish()
            return nc

        def quant_rows(buf, ncols, toks):
            t_mx = P.op("vector", lambda e, n=ncols: e.reduce_max(
                mx1[:, :], buf[0:128, 0:n], axis=AXX, apply_absolute_value=True),
                toks=toks, small=True)
            t_sc = P.op("vector", lambda e: e.tensor_scalar(
                sc1[:, :], mx1[:, :], 1.0 / QMAX, float(Q_EPS), ALU.mult, ALU.max),
                toks=[t_mx], small=True)
            t_rc = P.op("vector", lambda e: e.reciprocal(rc1[:, :], sc1[:, :]),
                        toks=[t_sc], small=True)
            t_ca = P.op("scalar", lambda e: e.activation(
                rc1a[:, :], rc1[:, :], AF.Identity), toks=[t_rc], small=True)
            t_cb = P.op("scalar", lambda e: e.activation(
                sc1a[:, :], sc1[:, :], AF.Identity), toks=[t_rc], small=True)
            t_r1 = P.op("vector", lambda e, n=ncols: e.tensor_scalar(
                buf[0:128, 0:n], buf[0:128, 0:n], rc1a[:, :], MAGIC, ALU.mult, ALU.add),
                toks=[t_ca])
            return P.op("vector", lambda e, n=ncols: e.tensor_scalar(
                prep_bf[0:128, 0:n], buf[0:128, 0:n], -MAGIC, sc1a[:, :],
                ALU.add, ALU.mult), toks=[t_r1, t_cb], force_wait=True)

        w1_st = []
        prev_ev = None
        for ti in range(11):
            buf = prep_in[ti % 2]
            war = w1_st[ti - 2] if ti >= 2 else (
                t_ccwo if DEV.get("serial_prep") else None)
            t_ld = P.dma("sync", buf[:, 0:D], w1_sl.ap()[128 * ti:128 * (ti + 1), :],
                         toks=[war])
            t_q = quant_rows(buf, D, [t_ld, prev_ev] if prev_ev else [t_ld])
            stag_war = w1_st[-1] if w1_st else t_ccwo
            trs = [P.op("tensor", lambda e, k=k: e.transpose(
                trPS[:, 128 * k:128 * (k + 1)], prep_bf[:, 128 * k:128 * (k + 1)],
                identb_s[:, :]),
                toks=[t_q] + ([prev_ev, stag_war] if k == 0 else []))
                for k in range(16)]
            prev_ev = P.op("vector", lambda e: e.tensor_copy(
                stag[:, 0:2048], trPS[:, 0:2048]), toks=trs)
            w1_st.append(P.dma(
                "gpsimd",
                ag_w1_in.ap().rearrange("(k p) j -> p k j", p=128)
                [:, :, 128 * ti:128 * (ti + 1)],
                stag[:, 0:2048].rearrange("p (k j) -> p k j", k=16),
                toks=[prev_ev]))
        if until == "p2a":
            _finish()
            return nc
        t_ccw1 = cc(5, ag_w1_in, ag_w1_out, w1_st)
        if until == "p2b":
            _finish()
            return nc

        w2_st = []
        t_ld_w2 = None
        for ti in range(2 if not DEV.get("w2_tiles") else DEV["w2_tiles"]):
            buf = prep_in[ti % 2]
            t_ld = P.dma("sync", buf[:, 0:I_FF],
                         w2_sl.ap()[128 * ti:128 * (ti + 1), :],
                         toks=[w1_st[-1]] if ti == 0 else [w2_st[-1]])
            t_ld_w2 = t_ld
            t_q = quant_rows(buf, I_FF, [t_ld, prev_ev])
            if DEV.get("w2_noquant"):
                t_q = prev_ev
            ev_toks = []
            if DEV.get("w2_notrans"):
                w2_st.append(t_ld)
                prev_ev = t_q
                continue
            stag_war2 = (w2_st[-1] if w2_st else w1_st[-1])
            for gi in range(3):
                k0 = 16 * gi
                n = min(16, 44 - k0)
                war = ev_toks[gi - 1] if gi >= 1 else prev_ev
                trg = [P.op("tensor", lambda e, kk=k0 + j, p=j: e.transpose(
                    trPS[:, 128 * p:128 * (p + 1)], prep_bf[:, 128 * kk:128 * (kk + 1)],
                    identb_s[:, :]),
                    toks=[t_q, war] + ([stag_war2] if gi == 0 else [])
                    if j == 0 else [])
                    for j in range(n)]
                ev_toks.append(P.op("vector", lambda e, k0=k0, n=n: e.tensor_copy(
                    stag[:, 128 * k0:128 * (k0 + n)],
                    trPS[:, 0:128 * n]), toks=trg))
            prev_ev = ev_toks[-1]
            if DEV.get("w2_nodma"):
                w2_st.append(t_ld)
                continue
            for q4 in range(4):
                kk0 = 11 * q4
                w2_st.append(P.dma(
                    "gpsimd",
                    bass.AP(ag_w2_in, 128 * ti + W2SL * 128 * kk0,
                            [[W2SL, 128], [W2SL * 128, 11], [1, 128]]),
                    stag[:, 128 * kk0:128 * (kk0 + 11)]
                    .rearrange("p (k j) -> p k j", k=11),
                    toks=[ev_toks[-1]]))
        if until == "p2c":
            _finish()
            return nc
        t_ccw2 = cc(6, ag_w2_in, ag_w2_out, w2_st)
        if until == "p3":
            _finish()
            return nc

        t_gm = P.dma("sync", gmod_s[:, :], ag_mod_out.ap(),
                     toks=[t_ccmod, t_ld_w2])
        prev_t = None
        t_md = None
        for r in range(8):
            mm3 = [P.op("tensor", lambda e, r=r, u=u: e.matmul(
                big4[0:1, 512 * u:512 * (u + 1)], selb[:, r:r + 1],
                gmod_s[:, 512 * u:512 * (u + 1)], start=True, stop=True),
                toks=[t_gm, t_sel, t_silT, prev_t, t_mev] if u == 0 else [])
                for u in range(3)]
            prev_t = P.op("vector", lambda e: e.tensor_copy(
                mod_sb[0:1, 0:1536], big4[0:1, 0:1536]), toks=mm3 + [t_mw])
            t_md = P.dma("gpsimd", modrow_d.ap()[0:1, MODSL * r:MODSL * (r + 1)],
                         mod_sb[0:1, 0:1536], toks=[prev_t])
            prev_t = t_md

        t_scb = bcast_row(scalb[:, :], scal.ap()[0:1, 0:1], 1, [t_md],
                          extra_mm=[t_mev])
        if until == "p4":
            _finish()
            return nc
        if until == "wcheck":
            P.dma("gpsimd", o_out.ap()[0:1024, 0:1408],
                  ag_w1_out.ap()[0:1024, 0:1408], toks=[t_ccw1])
            P.dma("gpsimd", o_out.ap()[0:1024, 1408:2048],
                  bass.AP(ag_w2_out, 0, [[640, 1024], [1, 640]]), toks=[t_ccw2])
            _finish()
            return nc

        qnt = rg("qnt", [128, D], BF16, 0)
        knt = rg("knt", [128, NKV * DK], BF16, 4)
        biasw = rg("biasw", [128, 512], F32, 5)
        biasw2 = rg("biasw2", [128, 512], F32, 7)
        vs = rg("vs", [128, 9 * 512], BF16, 9)
        q_tok = rg("q_tok", [128, 8 * D], BF16, 18)
        k_tok = rg("k_tok", [128, 9 * 512], BF16, 50)
        hT = rg("hT", [128, 9 * 2048], BF16, 59)
        wwina = rg("wwina", [128, 16 * 512], BF16, 95)
        wwinb = rg("wwinb", [128, 16 * 512], BF16, 111)
        xbuf = [rg(f"xbuf{i}", [128, D], F32, 127 + 8 * i) for i in range(2)]
        xn_s = rg("xn_s", [128, D], F32, 143)
        h_s = rg("h_s", [128, D], BF16, 151)
        hmu_s = rg("hmu_s", [128, D], BF16, 155)
        sqdump = rg("sqdump", [128, D], BF16, 159)
        vtmp = rg("vtmp", [128, 2 * 512], BF16, 163)
        qT = rg("qT", [128, 8 * 2048], BF16, 59)
        kT = rg("kT", [128, NKV * KTW], BF16, 91)
        ropeb = [rg(f"ropeb{i}", [128, NH * 64], F32, 101 + 4 * i) for i in range(4)]
        mscr = [rg(f"mscr{i}", [128, NH * 64], F32, 117 + 4 * i) for i in range(4)]
        sq_scr = rg("sq_scr", [128, D], F32, 133)
        attnT = rg("attnT", [128, 8 * 2048], BF16, 18)
        mask1_s = rg("mask1_s", [128, 2048], F32, 101)
        maskr_s = rg("maskr_s", [128, 2048], F32, 109)
        S_sb = rg("S_sb", [128, 2048], F32, 117)
        P_sb = rg("P_sb", [128, 2048], BF16, 125)
        PT_sb = rg("PT_sb", [128, 2048], BF16, 129)
        x2 = rg("x2", [128, 8 * D], F32, 50)
        wwoa = rg("wwoa", [128, 16 * 512], BF16, 114)
        wwob = rg("wwob", [128, 16 * 512], BF16, 130)
        xrel = [rg(f"xrel{i}", [128, D], F32, 146 + 8 * i) for i in range(2)]
        t1s = [rg(f"t1s{i}", [128, 512], F32, 162 + 2 * i) for i in range(2)]
        xqT = rg("xqT", [128, 4 * 2048], BF16, 0)
        w1ring = [rg(f"w1ring{i}", [128, 1024], BF16, 16 + 2 * i) for i in range(4)]
        w2ring = [rg(f"w2ring{i}", [128, 2048], BF16, 24 + 4 * i) for i in range(2)]
        h2_s = rg("h2_s", [128, D], F32, 32)
        hm2_s = rg("hm2_s", [128, D], BF16, 40)
        sgl4 = rg("sgl4", [128, 4 * 512], BF16, 44)
        rscr = rg("rscr", [128, 512], F32, 48)
        t1f = rscr
        hbuf = rg("hbuf", [128, 4 * I_FF], BF16, 114)
        hqT = [rg(f"hqT{i}", [128, 44 * 128], BF16, 158 + 11 * i) for i in range(2)]

        prev = t_scb
        qn_mms = []
        for u in range(4):
            t_l = P.dma("sync", tiny2[0:1, 0:512], qn16.ap()[0:1, 512 * u:512 * (u + 1)],
                        toks=[prev])
            prev = P.op("tensor", lambda e, u=u: e.matmul(
                big4[:, 512 * u:512 * (u + 1)], onesr_s[0:1, :], tiny2[0:1, 0:512],
                start=True, stop=True), toks=[t_l, t_scb])
            qn_mms.append(prev)
        t_qnt = P.op("vector", lambda e: e.tensor_scalar(
            qnt[:, :], big4[:, :], scalb[:, :], None, ALU.mult), toks=qn_mms)
        t_knl = P.dma("sync", tiny2[0:1, 0:512], kn4.ap(), toks=[t_qnt])
        kn_mm = P.op("tensor", lambda e: e.matmul(
            big4[:, 0:512], onesr_s[0:1, :], tiny2[0:1, 0:512], start=True, stop=True),
            toks=[t_knl, t_qnt])
        t_knt = P.op("vector", lambda e: e.tensor_copy(knt[:, :], big4[:, 0:512]),
                     toks=[kn_mm])

        def build_mod_tiles(which, toks_first):
            toks = list(toks_first)
            outs = []
            for j, logi in enumerate((0, 1, 2) if which == 0 else (3, 4, 5)):
                mms = []
                prev_mm = None
                for u in range(4):
                    t_ld = P.dma("sync", tiny2[0:1, 0:512],
                                 modrow_d.ap()[0:1, D * logi + 512 * u:
                                               D * logi + 512 * (u + 1)],
                                 toks=toks + [prev_mm])
                    src = t_ld
                    if j == 2:
                        t_abs = P.op("scalar", lambda e: e.activation(
                            tinyb[0:1, 0:512], tiny2[0:1, 0:512], AF.Abs), toks=[t_ld])
                        t_mx2 = P.op("vector", lambda e: e.tensor_scalar(
                            tinyb[0:1, 0:512], tinyb[0:1, 0:512], 0.25, None, ALU.max),
                            toks=[t_abs], small=True, force_wait=True)
                        t_ge = P.op("vector", lambda e: e.tensor_scalar(
                            tiny2[0:1, 0:512], tiny2[0:1, 0:512], 0.0, 2.0,
                            ALU.is_ge, ALU.mult),
                            toks=[t_mx2], small=True, force_wait=True)
                        t_sg = P.op("vector", lambda e: e.tensor_scalar(
                            tiny2[0:1, 0:512], tiny2[0:1, 0:512], -1.0, None, ALU.add),
                            toks=[t_ge], small=True, force_wait=True)
                        src = P.op("vector", lambda e: e.tensor_mul(
                            tiny2[0:1, 0:512], tiny2[0:1, 0:512], tinyb[0:1, 0:512]),
                            toks=[t_sg], small=True, force_wait=True)
                    prev_mm = P.op("tensor", lambda e, u=u: e.matmul(
                        big4[:, 512 * u:512 * (u + 1)], onesr_s[0:1, :],
                        tiny2[0:1, 0:512], start=True, stop=True),
                        toks=[src] + (toks if u == 0 else []))
                    mms.append(prev_mm)
                if j < 2:
                    t_cp = P.op("vector", lambda e, j=j: e.tensor_copy(
                        modt[j][:, :], big4[:, :]), toks=mms)
                else:
                    t_cp = P.op("vector", lambda e: e.tensor_copy(
                        gat_f[:, :], big4[:, :]), toks=mms)
                outs.append(t_cp)
                toks = [t_cp]
            return outs

        mt_toks = build_mod_tiles(0, [t_knt, t_md])

        if until == "prep":
            _finish()
            return nc

        def load_win_q(dst, G, w, toks):
            return [P.dma(
                "sync",
                dst[:, 0:16 * 512].rearrange("p (k j) -> p k j", k=16)
                [:, 8 * kh:8 * (kh + 1), 256 * rloc:256 * (rloc + 1)],
                G.ap().rearrange("(r k p) j -> p r k j", k=16, p=128)
                [:, 2 * w + rloc, 8 * kh:8 * (kh + 1), :],
                toks=toks) for rloc in range(2) for kh in range(2)]

        def load_win_kv(dst, G, toks):
            return [P.dma(
                "sync",
                dst[:, 512 * k:512 * (k + 1)].rearrange("p (r j) -> p r j", r=8),
                G.ap().rearrange("(r k p) j -> p k r j", k=16, p=128)[:, k, :, :],
                toks=toks) for k in range(16)]

        _serial = [t_ccw1, t_ccw2] if DEV.get("serial_prep") else []
        t_wk = load_win_kv(wwina, ag_wk_out, [t_ccwk, w2_st[-1]] + _serial)
        t_wv = load_win_kv(wwinb, ag_wv_out, [t_ccwv, w2_st[-1]])
        t_bk = bcast_row(biasw[:, 0:512], bk_r.ap()[0:1, 0:512], 512,
                         [mt_toks[-1]], extra_mm=[mt_toks[-1]])
        t_bv = bcast_row(biasw2[:, 0:512], bv_r.ap()[0:1, 0:512], 512, [t_bk])

        h_done = []
        hT_done = []
        kv_ev = {}
        vs_done = {}
        for ti in range(NT):
            buf = xbuf[ti % 2]
            war = [h_done[ti - 2]] if ti >= 2 else [w2_st[-1]]
            t_x = P.dma("sync", buf[:, :], xin.ap()[128 * ti:128 * (ti + 1), :],
                        toks=war)
            prev = [hT_done[ti - 1]] if ti >= 1 else (mt_toks + [w2_st[-1], t_scb])
            t_nm, t_rs = ln_chain(buf[:, :], xn_s[:, :], sqdump[:, :],
                                  [t_x] + prev, LN_EPS)
            t_xn = P.op("vector", lambda e, b=buf: e.tensor_scalar(
                xn_s[:, :], b[:, :], nmt[:, :], rstdt[:, :], ALU.add, ALU.mult),
                toks=[t_nm, t_rs])
            t_h1 = P.op("gpsimd", lambda e: e.tensor_mul(
                hmu_s[:, :], xn_s[:, :], modt[1][:, :]), toks=[t_xn], force_wait=True)
            t_h2 = P.op("gpsimd", lambda e: e.tensor_add(
                hmu_s[:, :], hmu_s[:, :], modt[0][:, :]), toks=[t_h1])
            t_h = P.op("vector", lambda e: e.tensor_add(
                h_s[:, :], hmu_s[:, :], xn_s[:, :]), toks=[t_h2])
            h_done.append(t_h)
            trs = [P.op("tensor", lambda e, k=k: e.transpose(
                trPS[:, 128 * k:128 * (k + 1)], h_s[:, 128 * k:128 * (k + 1)],
                identb_s[:, :]),
                toks=[t_h] + ([hT_done[ti - 1]] if (k == 0 and ti >= 1) else []))
                for k in range(16)]
            t_hT = P.op("vector", lambda e, ti=ti: e.tensor_copy(
                hT[:, 2048 * ti:2048 * (ti + 1)], trPS[:, 0:2048]), toks=trs)
            hT_done.append(t_hT)

            for which in ("k", "v"):
                wbuf = wwina if which == "k" else wwinb
                wtok = t_wk if which == "k" else t_wv
                sl = 0 if which == "k" else 512
                extra = [t_hT] + wtok
                prevkey = (ti, "k") if which == "v" else (ti - 1, "v")
                if prevkey in kv_ev:
                    extra.append(kv_ev[prevkey])
                mms = [P.op("tensor", lambda e, k=k, ti=ti, wb=wbuf, sl=sl: e.matmul(
                    acc2[:, sl:sl + 512],
                    hT[:, 2048 * ti + 128 * k:2048 * ti + 128 * (k + 1)],
                    wb[:, 512 * k:512 * (k + 1)],
                    start=(k == 0), stop=(k == 15)),
                    toks=extra if k == 0 else []) for k in range(16)]
                if which == "k":
                    kv_ev[(ti, "k")] = P.op("vector", lambda e, ti=ti: e.tensor_add(
                        k_tok[:, 512 * ti:512 * (ti + 1)], acc2[:, 0:512],
                        biasw[:, 0:512]), toks=mms + [t_bk])
                else:
                    kv_ev[(ti, "v")] = P.op("vector", lambda e, ti=ti: e.tensor_add(
                        vtmp[:, 512 * (ti % 2):512 * (ti % 2 + 1)], acc2[:, 512:1024],
                        biasw2[:, 0:512]),
                        toks=mms + [t_bv, vs_done.get(ti - 1)])
            if ti >= 1:
                pv = vtmp[:, 512 * ((ti - 1) % 2):512 * ((ti - 1) % 2 + 1)]
                cv = vtmp[:, 512 * (ti % 2):512 * (ti % 2 + 1)]
                P.op("vector", lambda e, pv=pv, ti=ti: e.tensor_copy(
                    vs[0:64, 512 * (ti - 1):512 * ti], pv[64:128, :]),
                    toks=[kv_ev[(ti - 1, "v")], kv_ev[(ti, "v")]])
                vs_done[ti] = P.op("vector", lambda e, cv=cv, ti=ti: e.tensor_copy(
                    vs[64:128, 512 * (ti - 1):512 * ti], cv[0:64, :]),
                    toks=[kv_ev[(ti, "v")]])
        t_vs9a = P.op("vector", lambda e: e.tensor_copy(
            vs[0:64, 512 * 8:512 * 9], vtmp[64:128, 0:512]),
            toks=[kv_ev[(8, "v")]])
        t_vs9b = P.op("vector", lambda e: e.memset(vs[64:128, 512 * 8:512 * 9], 0.0),
                      toks=[])
        vs_done[9] = t_vs9a

        q_ev = {}
        for w in range(4):
            wbuf = wwina if w % 2 == 0 else wwinb
            if w < 2:
                war = [kv_ev[(8, "k" if w == 0 else "v")], vs_done[9], t_vs9b]
            else:
                war = [q_ev[(w - 2, ti)] for ti in range(1, 9)]
            wtoks = load_win_q(wbuf, ag_wq_out, w, [t_ccwq] + war)
            bt = biasw if w % 2 == 0 else biasw2
            bias_war = ([kv_ev[(8, "k")]] if w == 0 else
                        [kv_ev[(8, "v")]] if w == 1 else [q_ev[(w - 2, 8)]])
            t_bq = bcast_row(bt[:, 0:512], bq_r.ap()[0:1, 512 * w:512 * (w + 1)], 512,
                             bias_war, extra_mm=bias_war)
            for ti in range(1, 9):
                slot = (ti - 1) % 3
                mm0 = wtoks + [q_ev.get((w, ti - 3)), q_ev.get((w - 1, ti)),
                               hT_done[ti]]
                mms = [P.op("tensor", lambda e, k=k, ti=ti, wb=wbuf, sl=slot: e.matmul(
                    big4[:, 512 * sl:512 * (sl + 1)],
                    hT[:, 2048 * ti + 128 * k:2048 * ti + 128 * (k + 1)],
                    wb[:, 512 * k:512 * (k + 1)],
                    start=(k == 0), stop=(k == 15)),
                    toks=mm0 if k == 0 else []) for k in range(16)]
                q_ev[(w, ti)] = P.op(
                    "vector", lambda e, ti=ti, w=w, sl=slot, bt=bt: e.tensor_add(
                        q_tok[:, 2048 * (ti - 1) + 512 * w:
                              2048 * (ti - 1) + 512 * (w + 1)],
                        big4[:, 512 * sl:512 * (sl + 1)], bt[:, 0:512]),
                    toks=mms + [t_bq])
        last_q = [q_ev[(3, ti)] for ti in range(1, 9)]
        last_q3 = q_ev[(3, 8)]

        if until == "A":
            _finish()
            return nc

        def rms_rope(tok_ap, nh, wt_ap, cbuf, sbuf_, toks):
            nhd = nh * DK
            t_sq = P.op("scalar", lambda e, n=nhd: e.activation(
                sq_scr[:, 0:n], tok_ap, AF.Square), toks=toks)
            t_ms = P.op("vector", lambda e, nh=nh, n=nhd: e.reduce_sum(
                ms16[:, 0:nh], sq_scr[:, 0:n].rearrange("p (h j) -> p h j", h=nh),
                axis=AXX), toks=[t_sq], small=True)
            t_l = P.op("scalar", lambda e, nh=nh: e.activation(
                ms16[:, 0:nh], ms16[:, 0:nh], AF.Ln, bias=epsrm_t[:, :],
                scale=1.0 / DK), toks=[t_ms, t_epsb], small=True)
            t_r = P.op("scalar", lambda e, nh=nh: e.activation(
                rs16[:, 0:nh], ms16[:, 0:nh], AF.Exp, scale=-0.5),
                toks=[t_l], small=True)
            t_n = [P.op("vector", lambda e, h=h: e.tensor_scalar(
                tok_ap[:, DK * h:DK * (h + 1)], tok_ap[:, DK * h:DK * (h + 1)],
                rs16[:, h:h + 1], None, ALU.mult),
                toks=[t_r] if h == 0 else []) for h in range(nh)]
            t_g = P.op("vector", lambda e: e.tensor_mul(tok_ap, tok_ap, wt_ap),
                       toks=t_n)
            if DEV.get("b_norope"):
                return t_g
            v3 = tok_ap.rearrange("p (h two j) -> p h two j", h=nh, two=2)
            qe = v3[:, :, 0, :]
            qo = v3[:, :, 1, :]
            cb3 = cbuf.rearrange("p (h j) -> p h j", h=nh)
            sb3 = sbuf_.rearrange("p (h j) -> p h j", h=nh)
            ms = [mscr[i][:, 0:nh * 64].rearrange("p (h j) -> p h j", h=nh)
                  for i in range(4)]
            ta = P.op("vector", lambda e: e.tensor_mul(ms[0], qe, cb3), toks=[t_g])
            tb = P.op("vector", lambda e: e.tensor_mul(ms[1], qo, sb3), toks=[])
            tc = P.op("vector", lambda e: e.tensor_mul(ms[2], qe, sb3), toks=[])
            td = P.op("vector", lambda e: e.tensor_mul(ms[3], qo, cb3), toks=[])
            te = P.op("vector", lambda e: e.tensor_sub(qe, ms[0], ms[1]),
                      toks=[ta, tb, tc, td])
            return P.op("vector", lambda e: e.tensor_add(qo, ms[2], ms[3]), toks=[te])

        qT_done = {}
        for ti in range(1, 9):
            cbuf = ropeb[(ti % 2) * 2]
            sbuf_ = ropeb[(ti % 2) * 2 + 1]
            war = [qT_done.get(ti - 2)] + (last_q if ti <= 2 else [])
            t_lc = P.dma("sync", cbuf[:, :], cosq_t.ap()[128 * ti:128 * (ti + 1), :],
                         toks=war)
            t_ls = P.dma("sync", sbuf_[:, :], sinq_t.ap()[128 * ti:128 * (ti + 1), :],
                         toks=[])
            tok_ap = q_tok[:, 2048 * (ti - 1):2048 * ti]
            tf = rms_rope(tok_ap, NH, qnt[:, :], cbuf[:, :], sbuf_[:, :],
                          [q_ev[(3, ti)], t_lc, t_ls, t_qnt])
            trs = [P.op("tensor", lambda e, k=k, ti=ti: e.transpose(
                trPS[:, 128 * k:128 * (k + 1)],
                q_tok[:, 2048 * (ti - 1) + 128 * k:2048 * (ti - 1) + 128 * (k + 1)],
                identb_s[:, :]),
                toks=[tf] + ([qT_done[ti - 1]] if (k == 0 and ti >= 2) else
                             ([last_q3] if k == 0 else []))) for k in range(16)]
            qT_done[ti] = P.op("vector", lambda e, ti=ti: e.tensor_copy(
                qT[:, 2048 * (ti - 1):2048 * ti], trPS[:, 0:2048]), toks=trs)

        kT_done = {}
        kT3 = kT[:, 0:NKV * KTW].rearrange("p (kv t) -> p kv t", kv=NKV)
        for ti in range(NT if not DEV.get("b_nok") else 0):
            cbuf = ropeb[(ti % 2) * 2]
            sbuf_ = ropeb[(ti % 2) * 2 + 1]
            war = [kT_done.get(ti - 2)] + ([qT_done[8]] if ti <= 1 else [])
            t_lc = P.dma("sync", cbuf[:, 0:NKV * 64],
                         cosk_t.ap()[128 * ti:128 * (ti + 1), :], toks=war)
            t_ls = P.dma("sync", sbuf_[:, 0:NKV * 64],
                         sink_t.ap()[128 * ti:128 * (ti + 1), :], toks=[])
            tok_ap = k_tok[:, 512 * ti:512 * (ti + 1)]
            tf = rms_rope(tok_ap, NKV, knt[:, :], cbuf[:, 0:NKV * 64],
                          sbuf_[:, 0:NKV * 64],
                          [kv_ev[(ti, "k")], t_lc, t_ls, t_knt, qT_done[8]])
            trs = [P.op("tensor", lambda e, kv=kv, ti=ti: e.transpose(
                trPS[:, 128 * kv:128 * (kv + 1)],
                k_tok[:, 512 * ti + 128 * kv:512 * ti + 128 * (kv + 1)],
                identb_s[:, :]),
                toks=[tf] + ([kT_done[ti - 1]] if (kv == 0 and ti >= 1) else
                             ([qT_done[8]] if kv == 0 else []))) for kv in range(NKV)]
            kT_done[ti] = P.op("vector", lambda e, ti=ti: e.tensor_copy(
                kT3[:, :, 128 * ti:128 * (ti + 1)],
                trPS[:, 0:512].rearrange("p (kv t) -> p kv t", kv=NKV)), toks=trs)
        t_kpad = P.op("vector", lambda e: e.memset(kT3[:, :, 1152:1216], 0.0),
                      toks=[kT_done[8]]) if not DEV.get("b_nok") else None

        if until == "B":
            _finish()
            return nc

        t_m1 = P.dma("sync", mask1_s[:, :], mask_1[:, :], toks=[kT_done[8]])
        t_mr = P.dma("sync", maskr_s[:, :], mask_r[:, :], toks=[])

        attnT_done = {}
        c_big4 = t_knt
        c_Srd = kT_done[8]
        c_Prd_t = None
        c_PTrd = None
        c_acc2 = t_knt
        c_trPS = t_kpad
        first_qk = [qT_done[8], kT_done[8], t_kpad, vs_done[9], t_vs9b, t_m1, t_mr,
                    last_q3]
        for ti in range(1, 9):
            mask_s = mask1_s if ti == 1 else maskr_s
            for g in range(2):
                w0 = 128 * ti - 64
                qkm = [P.op("tensor", lambda e, hl=hl, ti=ti, w0=w0, g=g: e.matmul(
                    big4[:, 256 * hl:256 * (hl + 1)],
                    qT[:, 2048 * (ti - 1) + 128 * (8 * g + hl):
                       2048 * (ti - 1) + 128 * (8 * g + hl + 1)],
                    kT3[:, (8 * g + hl) // NREP_C, w0:w0 + 256],
                    start=True, stop=True),
                    toks=(first_qk + [qT_done[ti], c_big4]) if hl == 0 else [])
                    for hl in range(8)]
                t_S = P.op("vector", lambda e, m=mask_s: e.tensor_add(
                    S_sb[:, :], big4[:, :], m[:, :]), toks=qkm + [c_Srd])
                c_big4 = t_S
                t_gm2 = P.op("vector", lambda e: e.reduce_max(
                    gmax[:, :], S_sb[:, :].rearrange("p (h j) -> p h j", h=8),
                    axis=AXX), toks=[t_S], small=True)
                t_ng = P.op("vector", lambda e: e.tensor_scalar(
                    ngmax[:, :], gmax[:, :], -1.0, None, ALU.mult),
                    toks=[t_gm2, c_Srd], small=True)
                t_exp = [P.op("scalar", lambda e, hl=hl: e.activation(
                    P_sb[:, 256 * hl:256 * (hl + 1)], S_sb[:, 256 * hl:256 * (hl + 1)],
                    AF.Exp, bias=ngmax[:, hl:hl + 1]),
                    toks=[t_ng, c_Prd_t] if hl == 0 else []) for hl in range(8)]
                c_Srd = t_exp[-1]
                t_gs = P.op("vector", lambda e: e.reduce_sum(
                    gsum[:, :], P_sb[:, :].rearrange("p (h j) -> p h j", h=8),
                    axis=AXX), toks=t_exp, small=True)
                t_gr = P.op("vector", lambda e: e.reciprocal(grec[:, :], gsum[:, :]),
                            toks=[t_gs], small=True)
                t_nrm = [P.op("vector", lambda e, hl=hl: e.tensor_scalar(
                    P_sb[:, 256 * hl:256 * (hl + 1)], P_sb[:, 256 * hl:256 * (hl + 1)],
                    grec[:, hl:hl + 1], None, ALU.mult),
                    toks=[t_gr] if hl == 0 else []) for hl in range(8)]
                ptt = []
                for hl in range(8):
                    ptt.append(P.op("tensor", lambda e, hl=hl: e.transpose(
                        trPS[:, 128 * hl:128 * (hl + 1)],
                        P_sb[:, 256 * hl:256 * hl + 128], identb_s[:, :]),
                        toks=[t_nrm[hl]] + ([c_trPS] if hl == 0 else [])))
                    ptt.append(P.op("tensor", lambda e, hl=hl: e.transpose(
                        trPS[:, 1024 + 128 * hl:1024 + 128 * (hl + 1)],
                        P_sb[:, 256 * hl + 128:256 * hl + 256], identb_s[:, :]),
                        toks=[]))
                c_Prd_t = ptt[-1]
                t_pt = P.op("vector", lambda e: e.tensor_copy(
                    PT_sb[:, :], trPS[:, 0:2048]), toks=ptt + [c_PTrd])
                c_trPS = t_pt
                avm = []
                for hl in range(8):
                    kv = (8 * g + hl) // NREP_C
                    avm.append(P.op("tensor", lambda e, hl=hl, kv=kv, ti=ti: e.matmul(
                        acc2[:, 128 * hl:128 * (hl + 1)],
                        vs[:, 512 * (ti - 1) + 128 * kv:512 * (ti - 1) + 128 * (kv + 1)],
                        PT_sb[:, 128 * hl:128 * (hl + 1)], start=True, stop=False),
                        toks=[t_pt, c_acc2] if hl == 0 else []))
                    avm.append(P.op("tensor", lambda e, hl=hl, kv=kv, ti=ti: e.matmul(
                        acc2[:, 128 * hl:128 * (hl + 1)],
                        vs[:, 512 * ti + 128 * kv:512 * ti + 128 * (kv + 1)],
                        PT_sb[:, 1024 + 128 * hl:1024 + 128 * (hl + 1)],
                        start=False, stop=True), toks=[]))
                c_PTrd = avm[-1]
                t_at = P.op("vector", lambda e, ti=ti, g=g: e.tensor_copy(
                    attnT[:, 2048 * (ti - 1) + 1024 * g:2048 * (ti - 1) + 1024 * (g + 1)],
                    acc2[:, 0:1024]), toks=avm)
                attnT_done[(ti, g)] = t_at
                c_acc2 = t_at
        last_attn = attnT_done[(8, 1)]

        if until == "C":
            _finish()
            return nc

        ow_done = {}
        t1_rd = {}
        for w in range(4):
            wbuf = wwoa if w % 2 == 0 else wwob
            war = ([ow_done[(w - 2, ti)] for ti in range(1, 9)] if w >= 2
                   else [last_attn])
            wtoks = load_win_q(wbuf, ag_wo_out, w, [t_ccwo] + war)
            bt = biasw if w % 2 == 0 else biasw2
            bias_war = [last_q3 if w < 2 else ow_done[(w - 2, 8)], c_big4]
            t_bo = bcast_row(bt[:, 0:512], bo_r.ap()[0:1, 512 * w:512 * (w + 1)], 512,
                             bias_war, extra_mm=bias_war)
            for ti in range(1, 9):
                slot = (ti - 1) % 3
                mm0 = wtoks + [ow_done.get((w, ti - 3)), ow_done.get((w - 1, ti)),
                               attnT_done[(ti, 1)], c_big4]
                mms = [P.op("tensor", lambda e, k=k, ti=ti, wb=wbuf, sl=slot: e.matmul(
                    big4[:, 512 * sl:512 * (sl + 1)],
                    attnT[:, 2048 * (ti - 1) + 128 * k:2048 * (ti - 1) + 128 * (k + 1)],
                    wb[:, 512 * k:512 * (k + 1)],
                    start=(k == 0), stop=(k == 15)),
                    toks=mm0 if k == 0 else []) for k in range(16)]
                par = (8 * w + ti) % 2
                t_e1 = P.op("vector", lambda e, sl=slot, bt=bt, par=par: e.tensor_add(
                    t1s[par][:, :], big4[:, 512 * sl:512 * (sl + 1)], bt[:, 0:512]),
                    toks=mms + [t_bo, t1_rd.get(par), last_attn])
                t_e2 = P.op("gpsimd", lambda e, par=par, w=w: e.tensor_mul(
                    t1s[par][:, :], t1s[par][:, :], gat_f[:, 512 * w:512 * (w + 1)]),
                    toks=[t_e1, last_attn], force_wait=True)
                t_e3 = P.op("gpsimd", lambda e, ti=ti, w=w, par=par: e.tensor_copy(
                    x2[:, 2048 * (ti - 1) + 512 * w:2048 * (ti - 1) + 512 * (w + 1)],
                    t1s[par][:, :]), toks=[t_e2])
                t1_rd[par] = t_e3
                ow_done[(w, ti)] = t_e3
        res_done = {}
        for ti in range(1, 9):
            slot = (ti - 1) % 2
            t_xr = P.dma("sync", xrel[slot][:, :], xin.ap()[128 * ti:128 * (ti + 1), :],
                         toks=[res_done.get(ti - 2), ow_done[(3, ti)]])
            res_done[ti] = P.op("vector", lambda e, ti=ti, slot=slot: e.tensor_add(
                x2[:, 2048 * (ti - 1):2048 * ti], x2[:, 2048 * (ti - 1):2048 * ti],
                xrel[slot][:, :]), toks=[t_xr, ow_done[(3, ti)]], force_wait=True)

        if until == "O":
            for ti in range(1, 9):
                P.dma("gpsimd", o_out.ap()[128 * (ti - 1):128 * ti, :],
                      x2[:, 2048 * (ti - 1):2048 * ti], toks=[res_done[ti]])
            _finish()
            return nc

        mt2_toks = build_mod_tiles(1, [res_done[8], t1_rd.get(0), t1_rd.get(1)])

        def w1_ranges(w):
            out = []
            o0 = 512 * w
            while o0 < 512 * (w + 1):
                r = o0 // W1SL
                j0 = o0 % W1SL
                ln = min(W1SL - j0, 512 * (w + 1) - o0)
                out.append((r, j0, ln, o0 - 512 * w))
                o0 += ln
            return out

        def load_w1_chunk(dst, w, up, kp, toks):
            return [P.dma(
                "sync",
                dst[:, 0:1024].rearrange("p (k j) -> p k j", k=2)
                [:, :, doff:doff + ln],
                ag_w1_out.ap().rearrange("(r k p) j -> p r k j", k=16, p=128)
                [:, r + (4 if up else 0), 2 * kp:2 * kp + 2, j0:j0 + ln],
                toks=toks) for (r, j0, ln, doff) in w1_ranges(w)]

        PSG = [big4[:, 0:512], big4[:, 1024:1536], acc2[:, 0:512], dn67[:, 0:512]]
        PSU = [big4[:, 512:1024], big4[:, 1536:2048], acc2[:, 512:1024],
               dn67[:, 512:1024]]
        DSL = [[big4[:, 512 * dw:512 * (dw + 1)] for dw in range(4)],
               [acc2[:, 0:512], acc2[:, 512:1024], dn67[:, 0:512], dn67[:, 512:1024]]]

        grp_last_out = None
        grp_gate0 = mt2_toks[-1]
        t1f_rd = None
        xqT_done = {}
        for grp in range(2):
            tis = [1 + 4 * grp + i for i in range(4)]
            for loc, ti in enumerate(tis):
                src = x2[:, 2048 * (ti - 1):2048 * ti]
                prev = ([xqT_done[tis[loc - 1]]] if loc > 0 else
                        mt2_toks + [grp_gate0])
                t_nm, t_rs = ln_chain(src, h2_s[:, :], hm2_s[:, :],
                                      [res_done[ti]] + prev, LN_EPS)
                t_xn = P.op("vector", lambda e, src=src: e.tensor_scalar(
                    h2_s[:, :], src, nmt[:, :], rstdt[:, :], ALU.add, ALU.mult),
                    toks=[t_nm, t_rs])
                t_h1 = P.op("gpsimd", lambda e: e.tensor_mul(
                    hm2_s[:, :], h2_s[:, :], modt[1][:, :]), toks=[t_xn],
                    force_wait=True)
                t_h2 = P.op("gpsimd", lambda e: e.tensor_add(
                    hm2_s[:, :], hm2_s[:, :], modt[0][:, :]), toks=[t_h1])
                t_h = P.op("vector", lambda e: e.tensor_add(
                    h2_s[:, :], hm2_s[:, :], h2_s[:, :]), toks=[t_h2],
                    force_wait=True)
                if until == "F0" and grp == 0 and loc == 0:
                    P.dma("gpsimd", o_out.ap()[0:128, :], h2_s[:, :], toks=[t_h])
                    P.dma("gpsimd", o_out.ap()[128:256, :],
                          x2[:, 0:2048], toks=[t_h])
                    P.dma("gpsimd", o_out.ap()[256:384, :], hm2_s[:, :], toks=[t_h])
                    _finish()
                    return nc
                t_mx = P.op("vector", lambda e: e.reduce_max(
                    mx1[:, :], h2_s[:, :], axis=AXX, apply_absolute_value=True),
                    toks=[t_h], small=True)
                t_sc = P.op("vector", lambda e: e.tensor_scalar(
                    sc1[:, :], mx1[:, :], 1.0 / QMAX, float(Q_EPS), ALU.mult, ALU.max),
                    toks=[t_mx], small=True)
                t_rc = P.op("vector", lambda e: e.reciprocal(rc1[:, :], sc1[:, :]),
                            toks=[t_sc], small=True)
                t_ca = P.op("scalar", lambda e: e.activation(
                    rc1a[:, :], rc1[:, :], AF.Identity), toks=[t_rc], small=True)
                t_cb = P.op("scalar", lambda e: e.activation(
                    sc1a[:, :], sc1[:, :], AF.Identity), toks=[t_rc], small=True)
                t_r1 = P.op("vector", lambda e: e.tensor_scalar(
                    h2_s[:, :], h2_s[:, :], rc1a[:, :], MAGIC, ALU.mult, ALU.add),
                    toks=[t_ca])
                t_r2 = P.op("vector", lambda e: e.tensor_scalar(
                    hm2_s[:, :], h2_s[:, :], -MAGIC, sc1a[:, :], ALU.add, ALU.mult),
                    toks=[t_r1, t_cb], force_wait=True)
                trs = [P.op("tensor", lambda e, k=k, loc=loc: e.transpose(
                    trPS[:, 128 * k:128 * (k + 1)], hm2_s[:, 128 * k:128 * (k + 1)],
                    identb_s[:, :]),
                    toks=[t_r2] + ([xqT_done.get(tis[loc - 1]) if loc > 0
                                    else grp_gate0] if k == 0 else []))
                    for k in range(16)]
                xqT_done[ti] = P.op("vector", lambda e, loc=loc: e.tensor_copy(
                    xqT[:, 2048 * loc:2048 * (loc + 1)], trPS[:, 0:2048]), toks=trs)
                if until == "F1" and grp == 0 and loc == 0:
                    P.dma("gpsimd", o_out.ap()[128:256, :], hm2_s[:, :],
                          toks=[t_r2, xqT_done[ti]])
                    P.dma("gpsimd", o_out.ap()[256:384, :], h2_s[:, :],
                          toks=[t_r2, xqT_done[ti]])

            hh_ev = {}
            mm_lastk = {}
            for w in range(11):
                for kp in range(8):
                    sg_ = w1ring[kp % 2]
                    su_ = w1ring[2 + kp % 2]
                    war = [mm_lastk.get((w, kp - 2)) if kp >= 2 else
                           mm_lastk.get((w - 1, kp + 6))]
                    if w == 0 and kp < 2:
                        war = [mt2_toks[-1], grp_last_out]
                    lg = load_w1_chunk(sg_, w, False, kp, war)
                    lu = load_w1_chunk(su_, w, True, kp, [])
                    if until == "F4" and grp == 0 and w == 0 and kp == 0:
                        P.dma("gpsimd", o_out.ap()[0:128, 0:1024], sg_[:, :],
                              toks=lg + lu)
                        P.dma("gpsimd", o_out.ap()[0:128, 1024:2048], su_[:, :],
                              toks=lg + lu)
                        _finish()
                        return nc
                    for loc in range(4):
                        ti = tis[loc]
                        t0 = lg + lu
                        if kp == 0:
                            t0 = t0 + [xqT_done[ti], hh_ev.get((w - 1, loc)),
                                       grp_gate0 if w == 0 else None]
                        P.op("tensor", lambda e, kp=kp, loc=loc, sg=sg_: e.matmul(
                            PSG[loc], xqT[:, 2048 * loc + 256 * kp:
                                          2048 * loc + 256 * kp + 128],
                            sg[:, 0:512], start=(kp == 0), stop=False),
                            toks=t0 if (kp == 0 or loc == 0) else [])
                        P.op("tensor", lambda e, kp=kp, loc=loc, sg=sg_: e.matmul(
                            PSG[loc], xqT[:, 2048 * loc + 256 * kp + 128:
                                          2048 * loc + 256 * (kp + 1)],
                            sg[:, 512:1024], start=False, stop=(kp == 7)), toks=[])
                        P.op("tensor", lambda e, kp=kp, loc=loc, su=su_: e.matmul(
                            PSU[loc], xqT[:, 2048 * loc + 256 * kp:
                                          2048 * loc + 256 * kp + 128],
                            su[:, 0:512], start=(kp == 0), stop=False), toks=[])
                        mm_lastk[(w, kp)] = P.op(
                            "tensor", lambda e, kp=kp, loc=loc, su=su_: e.matmul(
                                PSU[loc], xqT[:, 2048 * loc + 256 * kp + 128:
                                              2048 * loc + 256 * (kp + 1)],
                                su[:, 512:1024], start=False, stop=(kp == 7)),
                            toks=[])
                if until == "F3" and grp == 0 and w == 0:
                    t_dbg = P.op("vector", lambda e: e.tensor_copy(
                        rscr[:, :], PSG[0]), toks=[mm_lastk[(w, 7)]])
                    P.dma("gpsimd", o_out.ap()[0:128, 0:512], rscr[:, :],
                          toks=[t_dbg])
                    t_dbg2 = P.op("vector", lambda e: e.tensor_copy(
                        rscr[:, :], PSU[0]), toks=[t_dbg], force_wait=True)
                    P.dma("gpsimd", o_out.ap()[0:128, 512:1024], rscr[:, :],
                          toks=[t_dbg2])
                    _finish()
                    return nc
                for loc in range(4):
                    sg_ap = sgl4[:, 512 * loc:512 * (loc + 1)]
                    t_si = P.op("scalar", lambda e, loc=loc, sa=sg_ap: e.activation(
                        sa, PSG[loc], AF.Silu),
                        toks=[mm_lastk[(w, 7)], hh_ev.get((w - 1, loc))])
                    hh_ev[(w, loc)] = P.op("vector",
                                           lambda e, loc=loc, w=w, sa=sg_ap: e.tensor_mul(
                        hbuf[:, I_FF * loc + 512 * w:I_FF * loc + 512 * (w + 1)],
                        sa, PSU[loc]), toks=[t_si])

            all_hh = [hh_ev[(10, loc)] for loc in range(4)]
            if until == "F1" and grp == 0:
                P.dma("gpsimd", o_out.ap()[896:1024, 0:2048], xqT[:, 0:2048],
                      toks=all_hh)
                P.dma("gpsimd", o_out.ap()[0:128, 0:2048], hbuf[:, 0:2048],
                      toks=all_hh)
                P.dma("gpsimd", o_out.ap()[384:512, 0:2048],
                      hbuf[:, 2048:4096], toks=all_hh)
                P.dma("gpsimd", o_out.ap()[512:640, 0:2048],
                      modt[0][:, :], toks=all_hh)
                P.dma("gpsimd", o_out.ap()[640:768, 0:2048],
                      modt[1][:, :], toks=all_hh)
                P.dma("gpsimd", o_out.ap()[768:896, 0:2048],
                      gat_f[:, :], toks=all_hh)
                _finish()
                return nc
            pair_last_f1 = None
            for pair in range(2):
                locs = (2 * pair, 2 * pair + 1)
                hq_done = {}
                for li, loc in enumerate(locs):
                    t_mx = P.op("vector", lambda e, loc=loc: e.reduce_max(
                        mx1[:, :], hbuf[:, I_FF * loc:I_FF * (loc + 1)], axis=AXX,
                        apply_absolute_value=True),
                        toks=all_hh + [t1f_rd], small=True)
                    t_sc = P.op("vector", lambda e: e.tensor_scalar(
                        sc1[:, :], mx1[:, :], 1.0 / QMAX, float(Q_EPS),
                        ALU.mult, ALU.max), toks=[t_mx], small=True)
                    rtok = P.op("vector", lambda e: e.reciprocal(rc1[:, :], sc1[:, :]),
                                toks=[t_sc], small=True)
                    t_ca = P.op("scalar", lambda e: e.activation(
                        rc1a[:, :], rc1[:, :], AF.Identity), toks=[rtok], small=True)
                    t_cb = P.op("scalar", lambda e: e.activation(
                        sc1a[:, :], sc1[:, :], AF.Identity), toks=[rtok], small=True)
                    rtok = t_cb
                    for cw in range(11):
                        ha = hbuf[:, I_FF * loc + 512 * cw:I_FF * loc + 512 * (cw + 1)]
                        t_a = P.op("vector", lambda e, ha=ha: e.tensor_scalar(
                            rscr[:, :], ha, rc1a[:, :], MAGIC, ALU.mult, ALU.add),
                            toks=[rtok, t_ca])
                        rtok = P.op("vector", lambda e, ha=ha: e.tensor_scalar(
                            ha, rscr[:, :], -MAGIC, sc1a[:, :], ALU.add, ALU.mult),
                            toks=[t_a], force_wait=True)
                    tr_ev = []
                    for gi in range(3):
                        k0 = 16 * gi
                        n = min(16, 44 - k0)
                        tk = [rtok] if gi == 0 else [tr_ev[gi - 1]]
                        if gi == 0 and pair_last_f1 is not None:
                            tk = tk + [pair_last_f1]
                        trg = [P.op("tensor",
                                    lambda e, kk=k0 + j, p=j, loc=loc:
                                    e.transpose(
                            trPS[:, 128 * p:128 * (p + 1)],
                            hbuf[:, I_FF * loc + 128 * kk:I_FF * loc + 128 * (kk + 1)],
                            identb_s[:, :]), toks=tk if j == 0 else [])
                            for j in range(n)]
                        tr_ev.append(P.op(
                            "vector", lambda e, k0=k0, n=n, li=li: e.tensor_copy(
                                hqT[li][:, 128 * k0:128 * (k0 + n)],
                                trPS[:, 0:128 * n]), toks=trg))
                    hq_done[li] = tr_ev[-1]
                dmm = {}
                w2_tok = [None, None]
                for k in range(44):
                    ring = w2ring[k % 2]
                    war = [dmm.get((k - 2, 1, 3))]
                    if k < 2:
                        war = war + [mt2_toks[-1]]
                    w2_tok[k % 2] = P.dma(
                        "sync", ring[:, :].rearrange("p (r j) -> p r j", r=8),
                        bass.AP(ag_w2_out, W2SL * 128 * k,
                                [[W2SL, 128], [I_FF * W2SL, 8], [1, W2SL]]),
                        toks=[t_ccw2] + war)
                    for li in range(2):
                        for dw in range(4):
                            t0 = []
                            if li == 0 and dw == 0:
                                t0 = [w2_tok[k % 2]]
                                if k == 0:
                                    t0 += [hq_done[0], hq_done[1]]
                            dmm[(k, li, dw)] = P.op(
                                "tensor", lambda e, k=k, li=li, dw=dw, r=ring: e.matmul(
                                    DSL[li][dw], hqT[li][:, 128 * k:128 * (k + 1)],
                                    r[:, 512 * dw:512 * (dw + 1)],
                                    start=(k == 0), stop=(k == 43)), toks=t0)
                for li, loc in enumerate(locs):
                    ti = tis[loc]
                    fin = []
                    for dw in range(4):
                        t_f1 = P.op("vector", lambda e, li=li, dw=dw: e.tensor_mul(
                            t1f[:, :], DSL[li][dw],
                            gat_f[:, 512 * dw:512 * (dw + 1)]),
                            toks=[dmm[(43, li, dw)], t1f_rd])
                        t_f2 = P.op("gpsimd", lambda e, ti=ti, dw=dw: e.tensor_add(
                            x2[:, 2048 * (ti - 1) + 512 * dw:
                               2048 * (ti - 1) + 512 * (dw + 1)],
                            x2[:, 2048 * (ti - 1) + 512 * dw:
                               2048 * (ti - 1) + 512 * (dw + 1)],
                            t1f[:, :]), toks=[t_f1], force_wait=True)
                        t1f_rd = t_f2
                        fin.append(t_f2)
                        pair_last_f1 = t_f1
                    grp_last_out = P.dma("gpsimd",
                                         o_out.ap()[128 * (ti - 1):128 * ti, :],
                                         x2[:, 2048 * (ti - 1):2048 * ti], toks=fin)
            grp_gate0 = pair_last_f1

        _finish()
    return nc


NREP_C = NH // NKV


_NC_CACHE = {}


def _get_nc():
    if "nc" not in _NC_CACHE:
        _NC_CACHE["nc"] = build_nc()
    return _NC_CACHE["nc"]


def kernel(L0_attn_x, L0_attn_Wq, L0_attn_bq, L0_attn_Wk, L0_attn_bk, L0_attn_Wv,
           L0_attn_bv, L0_attn_Wo, L0_attn_bo, L0_attn_qn_w, L0_attn_kn_w,
           L0_attn_scale, L0_ffn_w_gate_up, L0_ffn_w_down, L0_c, L0_wmod):
    x = np.ascontiguousarray(np.asarray(L0_attn_x, dtype=np.float32))
    permq = _perm_full(NH)
    permk = _perm_full(NKV)
    permh = _perm_head()
    wq_p = np.ascontiguousarray(np.asarray(L0_attn_Wq, np.float32)[permq, :])
    wk_p = np.ascontiguousarray(np.asarray(L0_attn_Wk, np.float32)[permk, :])
    wv_p = np.ascontiguousarray(np.asarray(L0_attn_Wv, np.float32))
    wo_p = np.ascontiguousarray(np.asarray(L0_attn_Wo, np.float32))
    bq_p = np.ascontiguousarray(np.asarray(L0_attn_bq, np.float32)[permq][None, :])
    bk_p = np.ascontiguousarray(np.asarray(L0_attn_bk, np.float32)[permk][None, :])
    bv_p = np.ascontiguousarray(np.asarray(L0_attn_bv, np.float32)[None, :])
    bo_p = np.ascontiguousarray(np.asarray(L0_attn_bo, np.float32)[None, :])
    qn_p = np.ascontiguousarray(
        np.tile(np.asarray(L0_attn_qn_w, np.float32)[permh], NH)[None, :])
    kn_p = np.ascontiguousarray(
        np.tile(np.asarray(L0_attn_kn_w, np.float32)[permh], NKV)[None, :])
    scal = np.asarray(L0_attn_scale, np.float32).reshape(1, 1)
    w1 = np.ascontiguousarray(np.asarray(L0_ffn_w_gate_up, np.float32))
    w2 = np.ascontiguousarray(np.asarray(L0_ffn_w_down, np.float32))
    cfull = np.ascontiguousarray(np.asarray(L0_c, np.float32))
    wmod = np.ascontiguousarray(np.asarray(L0_wmod, np.float32))

    identb = np.eye(128, dtype=ml_dtypes.bfloat16)
    identf = np.eye(128, dtype=np.float32)
    onesr = np.ones((1, 128), dtype=np.float32)
    m1h = _mask_tile(True)
    mr = _mask_tile(False)

    in_maps = []
    for c in range(NCORE):
        b, half = c // 2, c % 2
        if half == 0:
            xs = np.concatenate([x[b, 0:128], x[b, 0:1024]], axis=0)
            s0 = -128
            mask1 = m1h
        else:
            xs = x[b, 896:2048]
            s0 = 896
            mask1 = mr
        cq, sq, ck, sk = _rope_tables(s0)
        selm = np.zeros((32, 8), dtype=np.float32)
        for r in range(8):
            selm[4 * r + b, r] = 1.0
        in_maps.append({
            "xin": np.ascontiguousarray(xs),
            "cfull": cfull,
            "wmod_sl": np.ascontiguousarray(wmod[:, MODSL * c:MODSL * (c + 1)]),
            "w1_sl": np.ascontiguousarray(w1[W1SL * c:W1SL * (c + 1), :]),
            "w2_sl": np.ascontiguousarray(w2[W2SL * c:W2SL * (c + 1), :]),
            "wq_sl": np.ascontiguousarray(wq_p[WQSL * c:WQSL * (c + 1), :]),
            "wk_sl": np.ascontiguousarray(wk_p[WKSL * c:WKSL * (c + 1), :]),
            "wv_sl": np.ascontiguousarray(wv_p[WKSL * c:WKSL * (c + 1), :]),
            "wo_sl": np.ascontiguousarray(wo_p[WQSL * c:WQSL * (c + 1), :]),
            "bq_r": bq_p, "bk_r": bk_p, "bv_r": bv_p, "bo_r": bo_p,
            "qn16": qn_p, "kn4": kn_p, "scal": scal, "sel": selm,
            "identb": identb, "identf": identf, "onesr": onesr,
            "cosq_t": cq, "sinq_t": sq, "cosk_t": ck, "sink_t": sk,
            "mask_1": mask1, "mask_r": mr,
        })

    nc = _get_nc()
    res = run_bass_kernel_spmd(nc, in_maps, core_ids=list(range(NCORE)))
    out = np.empty((B, S, D), dtype=np.float32)
    for c in range(NCORE):
        b, half = c // 2, c % 2
        out[b, 1024 * half:1024 * (half + 1), :] = np.asarray(res.results[c]["o_out"])
    return out
